# version 10
# baseline (speedup 1.0000x reference)
import numpy as np
import ml_dtypes
from contextlib import ExitStack
import concourse.bass as bass
import concourse.mybir as mybir
from concourse.bass_utils import run_bass_kernel_spmd

F32 = mybir.dt.float32
BF16 = mybir.dt.bfloat16
AF = mybir.ActivationFunctionType
OP = mybir.AluOpType
bf16 = ml_dtypes.bfloat16

D = 4096
NCH = 32
NCORE = 8

ENG_SEM = {"tensor": "sPE", "vector": "sDV", "scalar": "sAC",
           "gpsimd": "sGP", "sync": "sSY"}
Q_SEM = {"gpsimd": "qGP", "sync": "qSY", "scalar": "qAC"}


def _consts():
    c = {}
    jj = np.arange(D, dtype=np.float64)
    Om = np.exp(-2j * np.pi * jj / D)
    T0 = (1.0 - Om) / (1.0 + Om)
    cc = 2.0 / (1.0 + Om)
    c["cT0"] = ("f", np.concatenate([T0.real, T0.imag]).astype(np.float32).reshape(1, 2 * D))
    ccm = cc.reshape(NCH, 128).T
    c["cCre"] = ("f", ccm.real.astype(np.float32).copy())
    c["cCim"] = ("f", ccm.imag.astype(np.float32).copy())
    W32 = np.exp(2j * np.pi * np.outer(np.arange(32), np.arange(32)) / 32)
    c["cAre"] = ("f", W32.real.astype(np.float32).copy())
    c["cAim"] = ("f", W32.imag.astype(np.float32).copy())
    c["cAimN"] = ("f", (-W32.imag).astype(np.float32).copy())
    T2w = np.exp(2j * np.pi * np.outer(np.arange(128), np.arange(32)) / D)
    c["cT2re"] = ("f", T2w.real.astype(np.float32).copy())
    c["cT2im"] = ("f", T2w.imag.astype(np.float32).copy())
    W128 = np.exp(2j * np.pi * np.outer(np.arange(128), np.arange(128)) / 128) / D
    c["cBre"] = ("f", W128.real.astype(np.float32).copy())
    c["cBimN"] = ("f", (-W128.imag).astype(np.float32).copy())
    c["cIdf"] = ("f", np.eye(128, dtype=np.float32))
    c["cIdb"] = ("b", np.eye(128, dtype=np.float32).astype(bf16))
    c["cJb"] = ("b", np.eye(128, dtype=np.float32)[::-1].copy().astype(bf16))
    c["cOnesf"] = ("f", np.ones((1, 128), dtype=np.float32))
    c["cOnesb"] = ("b", np.ones((1, 128), dtype=bf16))
    c["cNOnesb"] = ("b", np.full((1, 128), -1.0, np.float32).astype(bf16))
    c["cOnesColb"] = ("b", np.ones((128, 1), dtype=bf16))
    return c


class Prog:

    def __init__(self):
        self.ops = []

    def add(self, eng, fn, reads=(), writes=(), ndma=0, tag=""):
        self.ops.append(dict(eng=eng, fn=fn, reads=tuple(reads),
                             writes=tuple(writes), ndma=ndma, tag=tag))

    def schedule(self):
        writers, readers = {}, {}
        for i, o in enumerate(self.ops):
            deps = set()
            for k in o["reads"]:
                deps.update(writers.get(k, ()))
            for k in o["writes"]:
                deps.update(writers.get(k, ()))
                deps.update(readers.get(k, ()))
            deps.discard(i)
            if o["eng"] == "tensor" and not o["ndma"]:
                deps = {d for d in deps
                        if not (self.ops[d]["eng"] == "tensor" and not self.ops[d]["ndma"])}
            o["deps"] = deps
            for k in o["reads"]:
                readers.setdefault(k, []).append(i)
            for k in o["writes"]:
                writers[k] = [i]
                readers[k] = []
        NL = 8
        rrc = {}
        last_on_lane = {}
        for i, o in enumerate(self.ops):
            if o["ndma"]:
                q = Q_SEM[o["eng"]]
                lane = rrc.get(q, 0) % NL
                rrc[q] = rrc.get(q, 0) + 1
                o["sem"] = q + str(lane)
                prev = last_on_lane.get(o["sem"])
                if prev is not None:
                    o["deps"].add(prev)
                last_on_lane[o["sem"]] = i
            else:
                o["sem"] = ENG_SEM[o["eng"]]
        needed = set()
        for o in self.ops:
            needed.update(o["deps"])
        cnt = {}
        for i, o in enumerate(self.ops):
            o["marked"] = (i in needed) or bool(o["ndma"])
            if o["marked"]:
                cnt[o["sem"]] = cnt.get(o["sem"], 0) + (16 * o["ndma"] if o["ndma"] else 1)
                o["val"] = cnt[o["sem"]]
        wm = {}
        for i, o in enumerate(self.ops):
            ws = {}
            for d in o["deps"]:
                p = self.ops[d]
                assert p["marked"], (i, d, p["tag"])
                s = p["sem"]
                ws[s] = max(ws.get(s, 0), p["val"])
            final = []
            for s, v in sorted(ws.items()):
                key = (o["eng"], s)
                if wm.get(key, 0) < v:
                    wm[key] = v
                    final.append((s, v))
            o["waits"] = final


def build(rows):
    RB = min(512, rows)
    n_rb = rows // RB
    NG = RB // 128
    CONSTS = _consts()

    nc = bass.Bass("TRN2", target_bir_lowering=False)
    P = Prog()
    es = ExitStack()

    IN = {}
    def din(name, shape, dtype=F32):
        IN[name] = nc.dram_tensor(name, list(shape), dtype, kind="ExternalInput")
    din("u", (rows, D))
    din("Lam", (64, 2)); din("pq", (1, 256)); din("Vc", (64, 128))
    din("Ct", (1, 128)); din("Bp", (64, 1)); din("aD", (1, 1)); din("lstep", (1, 1))
    din("Utop", (D, 64)); din("Stop", (1, 64)); din("Vtop", (64, D))
    din("Utail", (D, 4032)); din("Stail", (4032,)); din("Vtail", (4032, D))
    din("Ur", (D, 4)); din("Str", (1, 4)); din("Vr4", (4, D))
    din("alpha", (1, 64)); din("beta", (1, 64)); din("fD", (1, 4))
    din("Rm", (4, 4)); din("bias", (4096,))
    for n, (k, a) in CONSTS.items():
        din(n, a.shape, F32 if k == "f" else BF16)

    OUT = nc.dram_tensor("OUT", [rows, D], F32, kind="ExternalOutput")
    u_tm = nc.dram_tensor("u_tm", [rows * D], BF16)
    VhPad = nc.dram_tensor("VhPad", [D * D], BF16)
    UPad = nc.dram_tensor("UPad", [D * D], BF16)
    VT = nc.dram_tensor("VT", [NCH * D * 128], BF16)
    UT = nc.dram_tensor("UT", [NCH * D * 128], BF16)
    X2S = nc.dram_tensor("X2S", [D * rows], BF16)
    Kpad = nc.dram_tensor("Kpad", [4224], F32)

    def sb(name, p, f, dtype):
        return es.enter_context(nc.sbuf_tensor(name, [p, f], dtype))
    def psum(name, p, f, dtype=F32):
        return es.enter_context(nc.psum_tensor(name, [p, f], dtype))

    sA = sb("sA", 128, NCH * RB, BF16)
    sB = sb("sB", 128, NCH * RB, BF16)
    sG = sb("sG", 128, NCH * 128, BF16)
    sPan = [sb(f"sPan{j}", 128, D, BF16) for j in range(2)]
    sXq = [sb(f"sXq{j}", 128, RB, BF16) for j in range(4)]
    sSq = [sb(f"sSq{j}", 128, RB, BF16) for j in range(4)]
    sTmp = [sb(f"sTmp{j}", 128, RB, BF16) for j in range(2)]
    sRstd = sb("sRstd", 128, RB, BF16)
    sAmn = sb("sAmn", 128, RB, BF16)
    sXin = sb("sXin", 128, D, BF16)
    sYout = [sb(f"sYout{j}", 128, 2048, F32) for j in range(2)]
    sHF = sb("sHF", 128, 2048, F32)
    sHB = sb("sHB", 128, 4096, BF16)
    sVr4b = sb("sVr4b", 4, D, BF16)
    sUr32f = sb("sUr32f", 128, 128, F32)
    sUr32b = sb("sUr32b", 128, 128, BF16)
    kv = sb("kv", 128, 512, F32)
    wood = sb("wood", 128, 32 * 12, F32)
    sQ = sb("sQ", 4, D, BF16)
    sTd = sb("sTd", 4, RB, BF16)
    sVr = sb("sVr", 128, NCH * 4, BF16)
    sScol = sb("sScol", 128, NCH, F32)
    sBcol = sb("sBcol", 128, NCH, F32)
    sRows32 = sb("sRows32", 32, 128, F32)
    m2col = sb("m2col", 128, NG, F32)
    r2col = sb("r2col", 128, NG, F32)
    mrow = sb("mrow", 1, RB, F32)
    srow = sb("srow", 1, RB, F32)
    rstdr = sb("rstdr", 1, RB, F32)
    rstd_b = sb("rstd_b", 1, RB, BF16)
    amn_b = sb("amn_b", 1, RB, BF16)
    zrow = sb("zrow", 1, 128, F32)
    Dp1 = sb("Dp1", 128, 1, F32)
    vcraw = sb("vcraw", 64, 128, F32)
    vcre = sb("vcre", 64, 64, F32); vcim = sb("vcim", 64, 64, F32)
    vctre = sb("vctre", 64, 64, F32); vctim = sb("vctim", 64, 64, F32)
    bp = sb("bp", 64, 1, F32)
    brow = sb("brow", 1, 128, F32)
    ctraw = sb("ctraw", 1, 128, F32)
    pqraw = sb("pqraw", 1, 256, F32)
    vv8 = sb("vv8", 1, 512, F32)
    vscr = sb("vscr", 1, 64, F32)
    vvt = sb("vvt", 64, 8, F32)
    lam2 = sb("lam2", 64, 2, F32)
    nlam = sb("nlam", 64, 2, F32)
    lsr = sb("lsr", 1, 1, F32)
    s2r = sb("s2r", 1, 64, F32)
    T0q = sb("T0q", 1, 2048, F32)
    denRe = sb("denRe", 64, 1024, F32)
    denIm = sb("denIm", 64, 1024, F32)
    den2 = sb("den2", 64, 1024, F32)
    dscr = sB[:].bitcast(F32)[0:64, 0:1024]
    DSCRW = ("dscr", "B0", "B1", "B2", "B3")
    arre = sb("arre", 128, 32, F32); arim = sb("arim", 128, 32, F32)
    arTre = sb("arTre", 32, 128, F32); arTim = sb("arTim", 32, 128, F32)
    aare = sb("aare", 128, 32, F32); aaim = sb("aaim", 128, 32, F32)
    a2re = sb("a2re", 128, 32, F32); a2im = sb("a2im", 128, 32, F32)
    sKm = sb("sKm", 128, 32, F32)
    sal = sb("sal", 1, 64, F32); sbe = sb("sbe", 1, 64, F32)
    shead = sb("shead", 1, 64, F32)
    fd4 = sb("fd4", 4, 1, F32); sd4 = sb("sd4", 4, 1, F32)
    sRm = sb("sRm", 4, 4, F32); sRsb = sb("sRsb", 4, 4, BF16)
    ur_t = sb("ur_t", 4, 128, BF16)

    cbuf = {}
    for n, (k, a) in CONSTS.items():
        if n == "cT0":
            continue
        cbuf[n] = sb("c_" + n, a.shape[0], a.shape[1], F32 if k == "f" else BF16)

    psf = [psum(f"psf{i}", 128, 512, F32) for i in range(7)]
    psb_t = psum("psb_t", 128, 1024, BF16)
    PSK = [f"PS{i}" for i in range(7)]

    A = P.add

    def chA(ic):
        return sA[:, ic * RB:(ic + 1) * RB]
    def chB(rc):
        return sB[:, rc * RB:(rc + 1) * RB]

    def ld1(dst, src, key, tag=""):
        A("gpsimd", lambda g, d=dst, s=src: [g.dma_start(out=d, in_=s)],
          writes=(key,), ndma=1, tag=tag or ("ld-" + key))

    ld1(lam2[:], IN["Lam"][:, :], "lam2")
    ld1(pqraw[:], IN["pq"][:, :], "pqraw")
    ld1(vcraw[:], IN["Vc"][:, :], "vcraw")
    ld1(ctraw[:], IN["Ct"][:, :], "ctraw")
    ld1(bp[:], IN["Bp"][:, :], "bp")
    ld1(lsr[:], IN["lstep"][:, :], "lsr")
    ld1(sal[:], IN["alpha"][:, :], "sal")
    ld1(sbe[:], IN["beta"][:, :], "sbe")
    ld1(shead[:], IN["Stop"][:, :], "shead")
    ld1(sRm[:], IN["Rm"][:, :], "sRm")
    ld1(fd4[:], bass.AP(IN["fD"], 0, [[1, 4], [1, 1]]), "fd4")
    ld1(sd4[:], bass.AP(IN["Str"], 0, [[1, 4], [1, 1]]), "sd4")
    ld1(Dp1[0:1, 0:1], IN["aD"][:, :], "Dp1")
    for n in CONSTS:
        if n == "cT0":
            continue
        ld1(cbuf[n][:], IN[n][:, :], "c_" + n)


    A("gpsimd", lambda g: [g.dma_start(out=sVr4b[:], in_=IN["Vr4"][:, :])],
      writes=("sVr4b",), ndma=1, tag="vr4b")
    A("scalar", lambda s: [
        s.dma_start(out=bass.AP(sUr32f, q * 32, [[128, 128], [4, 8], [1, 4]]),
                    in_=bass.AP(IN["Ur"], q * 8 * 512, [[4, 128], [512, 8], [1, 4]]))
        for q in range(4)],
      writes=("sUr32f",), ndma=4, tag="ur32f")
    A("vector", lambda v: [v.tensor_copy(sUr32b[:], sUr32f[:])],
      reads=("sUr32f",), writes=("sUr32b",), tag="ur32cast")

    for k in range(4):
        c0, c1 = 1024 * k, 1024 * (k + 1)
        A("gpsimd", lambda g, c0=c0, c1=c1: [
            g.dma_start(out=bass.AP(VhPad, c0, [[D, 4032], [1, 1024]]),
                        in_=bass.AP(IN["Vtail"], c0, [[D, 4032], [1, 1024]])),
            g.dma_start(out=bass.AP(VhPad, 4032 * D + c0, [[D, 64], [1, 1024]]),
                        in_=bass.AP(IN["Vtop"], c0, [[D, 64], [1, 1024]])),
        ], writes=(f"VhPad{k}",), ndma=2, tag=f"castVh{k}")

    nq = 4 if rows % 4 == 0 and rows >= 512 else 1
    RQ = rows // nq
    for k in range(nq):
        A("gpsimd", lambda g, k=k: [g.dma_start(
            out=bass.AP(u_tm, k * RQ * D, [[D, RQ], [1, D]]),
            in_=IN["u"][k * RQ:(k + 1) * RQ, :])],
          writes=(f"u_tm{k}",), ndma=1, tag=f"castu{k}")

    for k in range(4):
        c0 = 1024 * k
        tw = 1024 if k < 3 else 4032 - 3072
        def _ucast(g, c0=c0, tw=tw, k=k):
            out = [g.dma_start(out=bass.AP(UPad, c0, [[D, D], [1, tw]]),
                               in_=bass.AP(IN["Utail"], c0, [[4032, D], [1, tw]]))]
            if k == 3:
                out.append(g.dma_start(out=bass.AP(UPad, 4032, [[D, D], [1, 64]]),
                                       in_=bass.AP(IN["Utop"], 0, [[64, D], [1, 64]])))
            return out
        A("gpsimd", _ucast, writes=(f"UPad{k}",), ndma=(2 if k == 3 else 1),
          tag=f"castU{k}")

    def u_transposes(rb):
        q0 = (rb * RB) // RQ
        q1 = ((rb + 1) * RB - 1) // RQ
        ukeys = tuple(f"u_tm{k}" for k in range(q0, q1 + 1))
        for ic in range(NCH):
            A("sync", lambda s, ic=ic, rb=rb: [s.dma_start(
                out=chA(ic),
                in_=bass.AP(u_tm, rb * RB * D + ic * 128, [[D, RB], [1, 128]]),
                transpose=True)],
              reads=ukeys, writes=(f"A{ic}",), ndma=1, tag=f"utr{rb}_{ic}")
    u_transposes(0)

    def tiled_prepass(SrcPad, Dst, dkey, tag, slot_aps, slot_keys):
        pk = "VhPad" if SrcPad is VhPad else "UPad"
        NS = len(slot_aps)
        def tr(ic):
            slot = ic % NS
            A("sync", lambda s, ic=ic, slot=slot: [s.dma_start(
                out=slot_aps[slot],
                in_=bass.AP(SrcPad, ic * 128, [[D, D], [1, 128]]), transpose=True)],
              reads=(pk + str(ic // 8),),
              writes=(slot_keys[slot],), ndma=1, tag=f"{tag}tr{ic}")
        def st(ic):
            slot = ic % NS
            A("sync", lambda s, ic=ic, slot=slot: [
                s.dma_start(
                    out=bass.AP(Dst, g8 * 8 * D * 128 + ic * 128,
                                [[4096, 128], [D * 128, 8], [1, 128]]),
                    in_=bass.AP(slot_aps[slot].tensor, slot_aps[slot].offset + g8 * 1024,
                                [[D, 128], [128, 8], [1, 128]]))
                for g8 in range(4)],
              reads=(slot_keys[slot],), writes=(dkey + str(ic),), ndma=4, tag=f"{tag}st{ic}")
        if NS >= 4:
            for ic in range(0, NCH, 4):
                tr(ic); tr(ic + 1); tr(ic + 2); tr(ic + 3)
                st(ic); st(ic + 1); st(ic + 2); st(ic + 3)
        else:
            for ic in range(0, NCH, 2):
                tr(ic); tr(ic + 1); st(ic); st(ic + 1)

    tiled_prepass(VhPad, VT, "VT", "vt",
                  [sPan[0][:], sPan[1][:], sXin[:], sYout[0][:].bitcast(BF16)],
                  ["pan0", "pan1", "XIN", "YO0"])
    tiled_prepass(UPad, UT, "UT", "ut",
                  [sXin[:], sYout[0][:].bitcast(BF16)], ["XIN", "YO0"])

    A("vector", lambda v: [v.tensor_scalar(Dp1[0:1, 0:1], Dp1[0:1, 0:1], 1.0, None, OP.add)],
      reads=("Dp1",), writes=("Dp1",))
    A("tensor", lambda t: [t.matmul(psf[6][0:128, 0:1], cbuf["cOnesf"][:, :], Dp1[0:1, 0:1],
                                    start=True, stop=True)],
      reads=("Dp1", "c_cOnesf"), writes=("PS6",))
    A("scalar", lambda s: [s.copy(Dp1[:, 0:1], psf[6][0:128, 0:1])],
      reads=("PS6",), writes=("Dp1",))


    A("scalar", lambda s: [s.activation(lsr[:], lsr[:], AF.Exp)],
      reads=("lsr",), writes=("lsr",))
    A("vector", lambda v: [v.reciprocal(lsr[:], lsr[:])],
      reads=("lsr",), writes=("lsr",))
    A("vector", lambda v: [v.tensor_scalar(lsr[:], lsr[:], 2.0, None, OP.mult)],
      reads=("lsr",), writes=("lsr",))
    A("vector", lambda v: [v.tensor_scalar(s2r[:], cbuf["cOnesf"][0:1, 0:64], lsr[:], None, OP.mult)],
      reads=("c_cOnesf", "lsr"), writes=("s2r",))
    A("vector", lambda v: [v.tensor_scalar(nlam[:], lam2[:], -1.0, None, OP.mult)],
      reads=("lam2",), writes=("nlam",))

    def emit_den(q):
        A("scalar", lambda s, q=q: [
            s.dma_start(out=T0q[0:1, 0:1024],
                        in_=bass.AP(IN["cT0"], q * 1024, [[2 * D, 1], [1, 1024]])),
            s.dma_start(out=T0q[0:1, 1024:2048],
                        in_=bass.AP(IN["cT0"], D + q * 1024, [[2 * D, 1], [1, 1024]])),
        ], writes=("T0q",), ndma=2, tag=f"T0q{q}")
        for h2 in range(2):
            sl = slice(h2 * 512, (h2 + 1) * 512)
            sl2 = slice(1024 + h2 * 512, 1024 + (h2 + 1) * 512)
            A("tensor", lambda t, sl=sl: [t.matmul(psf[0][0:64, 0:512], s2r[0:1, :],
                                                   T0q[0:1, sl], start=True, stop=True)],
              reads=("s2r", "T0q"), writes=("PS0",))
            A("scalar", lambda s, sl=sl: [s.activation(denRe[:, sl], psf[0][0:64, 0:512],
                                                       AF.Identity, bias=nlam[:, 0:1])],
              reads=("PS0", "nlam"), writes=("denRe",))
            A("tensor", lambda t, sl2=sl2: [t.matmul(psf[1][0:64, 0:512], s2r[0:1, :],
                                                     T0q[0:1, sl2], start=True, stop=True)],
              reads=("s2r", "T0q"), writes=("PS1",))
            A("scalar", lambda s, sl=sl: [s.activation(denIm[:, sl], psf[1][0:64, 0:512],
                                                       AF.Identity, bias=nlam[:, 1:2])],
              reads=("PS1", "nlam"), writes=("denIm",))
        A("vector", lambda v: [v.tensor_tensor(den2[:], denRe[:], denRe[:], OP.mult)],
          reads=("denRe",), writes=("den2",))
        A("vector", lambda v: [v.tensor_tensor(dscr, denIm[:], denIm[:], OP.mult)],
          reads=("denIm",), writes=DSCRW)
        A("vector", lambda v: [v.tensor_tensor(den2[:], den2[:], dscr, OP.add)],
          reads=("den2", "dscr"), writes=("den2",))
        A("vector", lambda v: [v.reciprocal(den2[:], den2[:])],
          reads=("den2",), writes=("den2",))
        A("vector", lambda v: [v.tensor_tensor(denRe[:], denRe[:], den2[:], OP.mult)],
          reads=("denRe", "den2"), writes=("denRe",))
        A("vector", lambda v: [v.scalar_tensor_tensor(denIm[:], denIm[:], -1.0, den2[:],
                                                      OP.mult, OP.mult)],
          reads=("denIm", "den2"), writes=("denIm",))

    def emit_cauchy(q):
        for tl in range(8):
            t_g = 8 * q + tl
            pk = 2 + (t_g % 2)
            A("tensor", lambda t, tl=tl, pk=pk: [
                t.matmul(psf[pk][0:128, 0:8], denRe[:, tl * 128:(tl + 1) * 128],
                         vvt[:], start=True, stop=True),
                t.matmul(psf[pk][0:128, 8:16], denIm[:, tl * 128:(tl + 1) * 128],
                         vvt[:], start=True, stop=True)],
              reads=("denRe", "denIm", "vvt"), writes=(PSK[pk],), tag=f"cau{t_g}")
            A("vector", lambda v, t_g=t_g, pk=pk: [
                v.tensor_copy(kv[:, t_g * 16:(t_g + 1) * 16], psf[pk][0:128, 0:16])],
              reads=(PSK[pk],), writes=("kv",), tag=f"cauev{t_g}")

    emit_den(0)
    A("vector", lambda v: [v.tensor_copy(vcre[:], bass.AP(vcraw, 0, [[128, 64], [2, 64]]))],
      reads=("vcraw",), writes=("vcre",))
    A("vector", lambda v: [v.tensor_copy(vcim[:], bass.AP(vcraw, 1, [[128, 64], [2, 64]]))],
      reads=("vcraw",), writes=("vcim",))
    A("tensor", lambda t: [t.transpose(psf[4][0:64, 0:64], vcre[:], cbuf["cIdf"][0:64, 0:64])],
      reads=("vcre", "c_cIdf"), writes=("PS4",))
    A("vector", lambda v: [v.tensor_copy(vctre[:], psf[4][0:64, 0:64])],
      reads=("PS4",), writes=("vctre",))
    A("tensor", lambda t: [t.transpose(psf[4][0:64, 0:64], vcim[:], cbuf["cIdf"][0:64, 0:64])],
      reads=("vcim", "c_cIdf"), writes=("PS4",))
    A("vector", lambda v: [v.tensor_copy(vctim[:], psf[4][0:64, 0:64])],
      reads=("PS4",), writes=("vctim",))
    A("tensor", lambda t: [t.matmul(psf[5][0:1, 0:64], bp[:], vctre[:], start=True, stop=True)],
      reads=("bp", "vctre"), writes=("PS5",))
    A("vector", lambda v: [v.tensor_copy(brow[0:1, 0:64], psf[5][0:1, 0:64])],
      reads=("PS5",), writes=("brow",))
    A("tensor", lambda t: [t.matmul(psf[5][0:1, 0:64], bp[:], vctim[:], start=True, stop=True)],
      reads=("bp", "vctim"), writes=("PS5",))
    A("vector", lambda v: [v.tensor_copy(brow[0:1, 64:128], psf[5][0:1, 0:64])],
      reads=("PS5",), writes=("brow",))

    A("vector", lambda v: [v.tensor_tensor(shead[:], shead[:], sal[:], OP.mult)],
      reads=("shead", "sal"), writes=("shead",))
    A("vector", lambda v: [v.tensor_tensor(shead[:], shead[:], sbe[:], OP.add)],
      reads=("shead", "sbe"), writes=("shead",))
    A("vector", lambda v: [v.tensor_relu(shead[:], shead[:])],
      reads=("shead",), writes=("shead",))

    ctre = bass.AP(ctraw, 0, [[128, 1], [2, 64]])
    ctim = bass.AP(ctraw, 1, [[128, 1], [2, 64]])
    pre_ = bass.AP(pqraw, 0, [[256, 1], [2, 64]])
    pim_ = bass.AP(pqraw, 1, [[256, 1], [2, 64]])
    qre_ = bass.AP(pqraw, 128, [[256, 1], [2, 64]])
    qim_ = bass.AP(pqraw, 129, [[256, 1], [2, 64]])
    bre_ = brow[0:1, 0:64]
    bim_ = brow[0:1, 64:128]

    def vprod(k, are, aim, xre, xim):
        o_re = vv8[0:1, k * 64:(k + 1) * 64]
        o_im = vv8[0:1, (k + 1) * 64:(k + 2) * 64]
        src = ("ctraw", "pqraw", "brow")
        A("vector", lambda v: [v.tensor_tensor(o_re, are, xre, OP.mult)],
          reads=src, writes=(f"vv{k}",))
        A("vector", lambda v: [v.tensor_tensor(vscr[:], aim, xim, OP.mult)],
          reads=src, writes=("vscr",))
        A("vector", lambda v: [v.tensor_tensor(o_re, o_re, vscr[:], OP.add)],
          reads=(f"vv{k}", "vscr"), writes=(f"vv{k}",))
        A("vector", lambda v: [v.tensor_tensor(o_im, are, xim, OP.mult)],
          reads=src, writes=(f"vv{k + 1}",))
        A("vector", lambda v: [v.tensor_tensor(vscr[:], aim, xre, OP.mult)],
          reads=src, writes=("vscr",))
        A("vector", lambda v: [v.tensor_tensor(o_im, o_im, vscr[:], OP.subtract)],
          reads=(f"vv{k + 1}", "vscr"), writes=(f"vv{k + 1}",))

    vprod(0, ctre, ctim, bre_, bim_)
    vprod(2, ctre, ctim, pre_, pim_)
    vprod(4, qre_, qim_, bre_, bim_)
    vprod(6, qre_, qim_, pre_, pim_)

    for k in range(8):
        A("scalar", lambda s, k=k: [s.dma_start(out=sRows32[k:k + 1, 0:64],
                                                in_=vv8[0:1, k * 64:(k + 1) * 64])],
          reads=(f"vv{k}",), writes=("sR32",), ndma=1, tag=f"vvasm{k}")
    A("tensor", lambda t: [t.transpose(psf[5][0:64, 0:8], sRows32[0:8, 0:64], cbuf["cIdf"][0:8, 0:8])],
      reads=("sR32", "c_cIdf"), writes=("PS5",))
    A("vector", lambda v: [v.tensor_copy(vvt[:], psf[5][0:64, 0:8])],
      reads=("PS5",), writes=("vvt",))
    emit_cauchy(0)
    for _q in range(1, 4):
        emit_den(_q)
        emit_cauchy(_q)

    def Am(m):
        return bass.AP(kv, m, [[512, 128], [16, 32]])
    def Bm(m):
        return bass.AP(kv, 8 + m, [[512, 128], [16, 32]])
    def W(k):
        return wood[:, 32 * k:32 * (k + 1)]

    def dve(fn, reads, writes, tag=""):
        A("vector", fn, reads=reads, writes=writes, tag=tag)

    dve(lambda v: [v.tensor_tensor(W(0), Am(0), Bm(1), OP.subtract)], ("kv",), ("wood",))
    dve(lambda v: [v.tensor_tensor(W(1), Bm(0), Am(1), OP.add)], ("kv",), ("wood",))
    dve(lambda v: [v.tensor_tensor(W(2), Am(2), Bm(3), OP.subtract)], ("kv",), ("wood",))
    dve(lambda v: [v.tensor_tensor(W(3), Bm(2), Am(3), OP.add)], ("kv",), ("wood",))
    dve(lambda v: [v.tensor_tensor(W(4), Am(4), Bm(5), OP.subtract)], ("kv",), ("wood",))
    dve(lambda v: [v.tensor_tensor(W(5), Bm(4), Am(5), OP.add)], ("kv",), ("wood",))
    dve(lambda v: [v.tensor_tensor(W(6), Am(6), Bm(7), OP.subtract)], ("kv",), ("wood",))
    dve(lambda v: [v.tensor_tensor(W(7), Bm(6), Am(7), OP.add)], ("kv",), ("wood",))
    dve(lambda v: [v.tensor_tensor(W(8), W(2), W(4), OP.mult)], ("wood",), ("wood",))
    dve(lambda v: [v.tensor_tensor(W(9), W(3), W(5), OP.mult)], ("wood",), ("wood",))
    dve(lambda v: [v.tensor_tensor(W(8), W(8), W(9), OP.subtract)], ("wood",), ("wood",))
    dve(lambda v: [v.tensor_tensor(W(9), W(2), W(5), OP.mult)], ("wood",), ("wood",))
    dve(lambda v: [v.tensor_tensor(W(10), W(3), W(4), OP.mult)], ("wood",), ("wood",))
    dve(lambda v: [v.tensor_tensor(W(9), W(9), W(10), OP.add)], ("wood",), ("wood",))
    dve(lambda v: [v.tensor_scalar(W(6), W(6), 1.0, None, OP.add)], ("wood",), ("wood",))
    dve(lambda v: [v.tensor_tensor(W(10), W(6), W(6), OP.mult)], ("wood",), ("wood",))
    dve(lambda v: [v.tensor_tensor(W(11), W(7), W(7), OP.mult)], ("wood",), ("wood",))
    dve(lambda v: [v.tensor_tensor(W(10), W(10), W(11), OP.add)], ("wood",), ("wood",))
    dve(lambda v: [v.reciprocal(W(10), W(10))], ("wood",), ("wood",))
    dve(lambda v: [v.tensor_tensor(W(11), W(8), W(6), OP.mult)], ("wood",), ("wood",))
    dve(lambda v: [v.tensor_tensor(W(2), W(9), W(7), OP.mult)], ("wood",), ("wood",))
    dve(lambda v: [v.tensor_tensor(W(11), W(11), W(2), OP.add)], ("wood",), ("wood",))
    dve(lambda v: [v.tensor_tensor(W(11), W(11), W(10), OP.mult)], ("wood",), ("wood",))
    dve(lambda v: [v.tensor_tensor(W(3), W(9), W(6), OP.mult)], ("wood",), ("wood",))
    dve(lambda v: [v.tensor_tensor(W(4), W(8), W(7), OP.mult)], ("wood",), ("wood",))
    dve(lambda v: [v.tensor_tensor(W(3), W(3), W(4), OP.subtract)], ("wood",), ("wood",))
    dve(lambda v: [v.tensor_tensor(W(3), W(3), W(10), OP.mult)], ("wood",), ("wood",))
    dve(lambda v: [v.tensor_tensor(W(0), W(0), W(11), OP.subtract)], ("wood",), ("wood",))
    dve(lambda v: [v.tensor_tensor(W(1), W(1), W(3), OP.subtract)], ("wood",), ("wood",))
    dve(lambda v: [v.tensor_tensor(arre[:], cbuf["cCre"][:], W(0), OP.mult)],
        ("wood", "c_cCre"), ("arre",))
    dve(lambda v: [v.tensor_tensor(W(4), cbuf["cCim"][:], W(1), OP.mult)], ("wood", "c_cCim"), ("wood",))
    dve(lambda v: [v.tensor_tensor(arre[:], arre[:], W(4), OP.subtract)], ("arre", "wood"), ("arre",))
    dve(lambda v: [v.tensor_tensor(arim[:], cbuf["cCre"][:], W(1), OP.mult)],
        ("wood", "c_cCre"), ("arim",))
    dve(lambda v: [v.tensor_tensor(W(5), cbuf["cCim"][:], W(0), OP.mult)], ("wood", "c_cCim"), ("wood",))
    dve(lambda v: [v.tensor_tensor(arim[:], arim[:], W(5), OP.add)], ("arim", "wood"), ("arim",))

    A("tensor", lambda t: [t.transpose(psf[4][0:32, 0:128], arre[:], cbuf["cIdf"][:, :])],
      reads=("arre", "c_cIdf"), writes=("PS4",))
    A("vector", lambda v: [v.tensor_copy(arTre[:], psf[4][0:32, 0:128])],
      reads=("PS4",), writes=("arTre",))
    A("tensor", lambda t: [t.transpose(psf[4][0:32, 0:128], arim[:], cbuf["cIdf"][:, :])],
      reads=("arim", "c_cIdf"), writes=("PS4",))
    A("vector", lambda v: [v.tensor_copy(arTim[:], psf[4][0:32, 0:128])],
      reads=("PS4",), writes=("arTim",))
    A("tensor", lambda t: [
        t.matmul(psf[0][0:128, 0:32], arTre[:], cbuf["cAre"][:, :], start=True, stop=False),
        t.matmul(psf[0][0:128, 0:32], arTim[:], cbuf["cAimN"][:, :], start=False, stop=True)],
      reads=("arTre", "arTim", "c_cAre", "c_cAimN"), writes=("PS0",))
    A("vector", lambda v: [v.tensor_copy(aare[:], psf[0][0:128, 0:32])],
      reads=("PS0",), writes=("aare",))
    A("tensor", lambda t: [
        t.matmul(psf[1][0:128, 0:32], arTre[:], cbuf["cAim"][:, :], start=True, stop=False),
        t.matmul(psf[1][0:128, 0:32], arTim[:], cbuf["cAre"][:, :], start=False, stop=True)],
      reads=("arTre", "arTim", "c_cAre", "c_cAim"), writes=("PS1",))
    A("vector", lambda v: [v.tensor_copy(aaim[:], psf[1][0:128, 0:32])],
      reads=("PS1",), writes=("aaim",))
    dve(lambda v: [v.tensor_tensor(a2re[:], aare[:], cbuf["cT2re"][:, :], OP.mult)],
        ("aare", "c_cT2re"), ("a2re",))
    dve(lambda v: [v.tensor_tensor(a2im[:], aaim[:], cbuf["cT2im"][:, :], OP.mult)],
        ("aaim", "c_cT2im"), ("a2im",))
    dve(lambda v: [v.tensor_tensor(a2re[:], a2re[:], a2im[:], OP.subtract)],
        ("a2re", "a2im"), ("a2re",))
    dve(lambda v: [v.tensor_tensor(a2im[:], aare[:], cbuf["cT2im"][:, :], OP.mult)],
        ("aare", "c_cT2im"), ("a2im",))
    dve(lambda v: [v.tensor_tensor(sKm[:], aaim[:], cbuf["cT2re"][:, :], OP.mult)],
        ("aaim", "c_cT2re"), ("sKm",))
    dve(lambda v: [v.tensor_tensor(a2im[:], a2im[:], sKm[:], OP.add)],
        ("a2im", "sKm"), ("a2im",))
    A("tensor", lambda t: [
        t.matmul(psf[2][0:128, 0:32], cbuf["cBre"][:, :], a2re[:], start=True, stop=False),
        t.matmul(psf[2][0:128, 0:32], cbuf["cBimN"][:, :], a2im[:], start=False, stop=True)],
      reads=("a2re", "a2im", "c_cBre", "c_cBimN"), writes=("PS2",))
    A("vector", lambda v: [v.tensor_copy(sKm[:], psf[2][0:128, 0:32])],
      reads=("PS2",), writes=("sKm",))

    A("vector", lambda v: [v.memset(zrow[:], 0.0)], writes=("zrow",))
    A("scalar", lambda s: [s.dma_start(out=bass.AP(Kpad, 0, [[128, 1], [1, 128]]),
                                       in_=zrow[:])],
      reads=("zrow",), writes=("Kpad",), ndma=1, tag="kpad0")
    A("scalar", lambda s: [s.dma_start(out=bass.AP(Kpad, 128, [[32, 128], [1, 32]]),
                                       in_=sKm[:])],
      reads=("sKm",), writes=("Kpad",), ndma=1, tag="kpadK")

    A("scalar", lambda s: [
        s.dma_start(out=sRows32[0:31, :], in_=bass.AP(IN["Stail"], 0, [[128, 31], [1, 128]])),
        s.dma_start(out=sRows32[31:32, 0:64], in_=bass.AP(IN["Stail"], 3968, [[64, 1], [1, 64]])),
    ], writes=("sR32",), ndma=2, tag="stailld")
    A("scalar", lambda s: [s.dma_start(out=sRows32[31:32, 64:128], in_=shead[0:1, 0:64])],
      reads=("shead",), writes=("sR32",), ndma=1, tag="sheadmv")
    A("tensor", lambda t: [t.transpose(psf[6][0:128, 0:32], sRows32[:, :], cbuf["cIdf"][0:32, 0:32])],
      reads=("sR32", "c_cIdf"), writes=("PS6",))
    A("vector", lambda v: [v.tensor_copy(sScol[:], psf[6][0:128, 0:32])],
      reads=("PS6",), writes=("sScol",))
    A("scalar", lambda s: [s.dma_start(out=sRows32[:, :],
                                       in_=bass.AP(IN["bias"], 0, [[128, 32], [1, 128]]))],
      reads=("sR32",), writes=("sR32",), ndma=1, tag="biasld")
    A("tensor", lambda t: [t.transpose(psf[6][0:128, 0:32], sRows32[:, :], cbuf["cIdf"][0:32, 0:32])],
      reads=("sR32", "c_cIdf"), writes=("PS6",))
    A("vector", lambda v: [v.tensor_copy(sBcol[:], psf[6][0:128, 0:32])],
      reads=("PS6",), writes=("sBcol",))
    A("vector", lambda v: [v.tensor_relu(fd4[:], fd4[:])], reads=("fd4",), writes=("fd4",))
    A("vector", lambda v: [v.tensor_tensor(sd4[:], sd4[:], fd4[:], OP.mult)],
      reads=("sd4", "fd4"), writes=("sd4",))
    A("vector", lambda v: [v.tensor_scalar(sRsb[:], sRm[:], sd4[:, 0:1], None, OP.mult)],
      reads=("sRm", "sd4"), writes=("sRsb",))
    for ic in range(NCH):
        A("tensor", lambda t, ic=ic: [t.transpose(psb_t[0:128, 0:4],
                                                  sVr4b[0:4, ic * 128:(ic + 1) * 128],
                                                  cbuf["cIdb"][0:4, 0:4])],
          reads=("sVr4b", "c_cIdb"), writes=("PSB",))
        A("vector", lambda v, ic=ic: [v.tensor_copy(sVr[:, ic * 4:(ic + 1) * 4], psb_t[0:128, 0:4])],
          reads=("PSB",), writes=("sVr",), tag=f"vrT{ic}")
        A("tensor", lambda t, ic=ic: [t.transpose(psb_t[0:4, 0:128],
                                                  sUr32b[:, ic * 4:(ic + 1) * 4],
                                                  cbuf["cIdb"][:, :])],
          reads=("sUr32b", "c_cIdb"), writes=("PSB",))
        A("vector", lambda v: [v.tensor_copy(ur_t[:], psb_t[0:4, 0:128])],
          reads=("PSB",), writes=("ur_t",))
        A("tensor", lambda t, ic=ic: [t.matmul(psf[6][0:4, 0:128], sRsb[:], ur_t[:],
                                               start=True, stop=True)],
          reads=("sRsb", "ur_t"), writes=("PS6",))
        A("vector", lambda v, ic=ic: [v.tensor_copy(sQ[:, ic * 128:(ic + 1) * 128],
                                                    psf[6][0:4, 0:128])],
          reads=("PS6",), writes=("sQ",), tag=f"Qev{ic}")

    for h2 in range(2):
        A("scalar", lambda s, h2=h2: [
            s.dma_start(out=bass.AP(sHF, g * 1024, [[2048, 128], [128, 8], [1, 128]]),
                        in_=bass.AP(Kpad, 1 + (16 * h2 + 8 * g) * 128,
                                    [[1, 128], [128, 8], [1, 128]]))
            for g in range(2)],
          reads=("Kpad",), writes=("sHF",), ndma=2, tag=f"Hld{h2}")
        A("vector", lambda v, h2=h2: [v.tensor_copy(sHB[:, h2 * 2048:(h2 + 1) * 2048], sHF[:])],
          reads=("sHF",), writes=("sHB",), tag=f"Hcast{h2}")
        for q4 in range(4):
            d0 = 16 * h2 + 4 * q4
            A("tensor", lambda t, d0=d0, h2=h2, q4=q4: [
                t.matmul(psf[q4 % 2][:, j * 128:(j + 1) * 128], cbuf["cJb"][:, :],
                         sHB[:, (16 * h2 + 4 * q4 + j) * 128:(16 * h2 + 4 * q4 + j + 1) * 128],
                         start=True, stop=True)
                for j in range(4)],
              reads=("sHB", "c_cJb"), writes=(PSK[q4 % 2],), tag=f"Gmm{d0}")
            A("vector", lambda v, d0=d0, q4=q4: [v.tensor_copy(
                sG[:, d0 * 128:(d0 + 4) * 128], psf[q4 % 2][:, 0:512])],
              reads=(PSK[q4 % 2],), writes=("G",), tag=f"Gev{d0}")





    for rb in range(n_rb):
        LAG = 16
        def emit_stats(b):
            A("tensor", lambda t, b=b: [t.matmul(psf[3][0:1, 0:RB], cbuf["cOnesColb"][:, :],
                                                 chB(b), start=(b == 0), stop=(b == NCH - 1),
                                                 skip_group_check=True)],
              reads=(f"B{b}", "c_cOnesColb"), writes=("PS3",), tag=f"summ{rb}_{b}")
            A("tensor", lambda t, b=b: [t.matmul(psf[5][0:1, 0:RB], cbuf["cOnesColb"][:, :],
                                                 sSq[b % 4][:, 0:RB], start=(b == 0),
                                                 stop=(b == NCH - 1), skip_group_check=True)],
              reads=(f"SQ{b % 4}", "c_cOnesColb"), writes=("PS5",), tag=f"sqm{rb}_{b}")
        for b in range(NCH):
            pk = b % 3
            def conv_mm(t, b=b, pk=pk):
                out = []
                for a in range(b + 1):
                    out.append(t.matmul(psf[pk][:, 0:RB], sG[:, (b - a) * 128:(b - a + 1) * 128],
                                        chA(a), start=(a == 0), stop=(a == b)))
                return out
            A("tensor", conv_mm,
              reads=("G",) + tuple(f"A{a}" for a in range(b + 1)),
              writes=(PSK[pk],), tag=f"conv{rb}_{b}")
            A("vector", lambda v, b=b, pk=pk: [v.scalar_tensor_tensor(
                chB(b), chA(b), Dp1[:, 0:1], psf[pk][:, 0:RB], OP.mult, OP.add)],
              reads=(f"A{b}", "Dp1", PSK[pk]), writes=(f"B{b}",), tag=f"cvev{rb}_{b}")
            A("scalar", lambda s, b=b: [s.activation(sSq[b % 4][:, 0:RB], chB(b), AF.Square)],
              reads=(f"B{b}",), writes=(f"SQ{b % 4}",), tag=f"sq{rb}_{b}")
            if b >= LAG:
                emit_stats(b - LAG)
        for b in range(NCH - LAG, NCH):
            emit_stats(b)
        ln_scalars_done = False
        A("vector", lambda v: [v.tensor_scalar(mrow[:], psf[3][0:1, 0:RB], 1.0 / D, None, OP.mult)],
          reads=("PS3",), writes=("mrow",), tag=f"m1{rb}")
        A("vector", lambda v: [v.tensor_scalar(srow[:], psf[5][0:1, 0:RB], 1.0 / D, None, OP.mult)],
          reads=("PS5",), writes=("srow",), tag=f"s1{rb}")
        A("vector", lambda v: [v.tensor_tensor(rstdr[:], mrow[:], mrow[:], OP.mult)],
          reads=("mrow",), writes=("rstdr",))
        A("vector", lambda v: [v.scalar_tensor_tensor(srow[:], srow[:], 1e-5, rstdr[:],
                                                      OP.add, OP.subtract)],
          reads=("srow", "rstdr"), writes=("srow",))
        A("scalar", lambda s: [s.activation(srow[:], srow[:], AF.Ln)],
          reads=("srow",), writes=("srow",))
        A("scalar", lambda s: [s.activation(rstdr[:], srow[:], AF.Exp, scale=-0.5)],
          reads=("srow",), writes=("rstdr",))
        A("vector", lambda v: [v.tensor_copy(rstd_b[:], rstdr[:])],
          reads=("rstdr",), writes=("rstd_b",))
        A("vector", lambda v: [v.tensor_tensor(srow[:], mrow[:], rstdr[:], OP.mult)],
          reads=("mrow", "rstdr"), writes=("srow",))
        A("vector", lambda v: [v.tensor_copy(amn_b[:], srow[:])],
          reads=("srow",), writes=("amn_b",))
        A("tensor", lambda t: [t.matmul(psf[0][:, 0:RB], cbuf["cOnesb"][:, :],
                                        rstd_b[0:1, 0:RB], start=True, stop=True)],
          reads=("rstd_b", "c_cOnesb"), writes=("PS0",), tag=f"bc1{rb}")
        A("tensor", lambda t: [t.matmul(psf[1][:, 0:RB], cbuf["cNOnesb"][:, :],
                                        amn_b[0:1, 0:RB], start=True, stop=True)],
          reads=("amn_b", "c_cNOnesb"), writes=("PS1",), tag=f"bc2{rb}")
        A("scalar", lambda s: [s.copy(sRstd[:, 0:RB], psf[0][:, 0:RB])],
          reads=("PS0",), writes=("sRstd",))
        A("scalar", lambda s: [s.copy(sAmn[:, 0:RB], psf[1][:, 0:RB])],
          reads=("PS1",), writes=("sAmn",))
        for ic in range(NCH):
            A("gpsimd", lambda g, ic=ic: [g.tensor_tensor(sTmp[ic % 2][:, 0:RB], chB(ic),
                                                          sRstd[:, 0:RB], OP.mult)],
              reads=(f"B{ic}", "sRstd"), writes=(f"TMP{ic % 2}",), tag=f"ap1_{rb}_{ic}")
            A("vector", lambda v, ic=ic: [v.tensor_tensor(chA(ic), sTmp[ic % 2][:, 0:RB],
                                                          sAmn[:, 0:RB], OP.add)],
              reads=(f"TMP{ic % 2}", "sAmn"), writes=(f"A{ic}",), tag=f"ap2_{rb}_{ic}")

        for rc in range(NCH):
            slot = rc % 2
            A("scalar", lambda s, rc=rc, slot=slot: [s.dma_start(
                out=sPan[slot][:], in_=bass.AP(VT, rc * D * 128, [[4096, 128], [1, 4096]]))],
              reads=tuple(f"VT{i}" for i in range(NCH)),
              writes=(f"pan{slot}",), ndma=1, tag=f"t1p{rb}_{rc}")
            pk = rc % 3
            for ic in range(NCH):
                A("tensor", lambda t, rc=rc, slot=slot, pk=pk, ic=ic: [
                    t.matmul(psf[pk][:, 0:RB],
                             sPan[slot][:, ic * 128:(ic + 1) * 128],
                             chA(ic), start=(ic == 0), stop=(ic == NCH - 1))],
                  reads=(f"pan{slot}", f"A{ic}"),
                  writes=(PSK[pk],), tag=f"t1mm{rb}_{rc}_{ic}")
            A("vector", lambda v, rc=rc, pk=pk: [v.tensor_scalar(
                chB(rc), psf[pk][:, 0:RB], sScol[:, rc:rc + 1], None, OP.mult)],
              reads=(PSK[pk], "sScol"), writes=(f"B{rc}",), tag=f"t1ev{rb}_{rc}")
        def td_mm(t):
            out = []
            for ic in range(NCH):
                out.append(t.matmul(psf[4][0:4, 0:RB], sVr[:, ic * 4:(ic + 1) * 4],
                                    chA(ic), start=(ic == 0), stop=(ic == NCH - 1)))
            return out
        A("tensor", td_mm, reads=("sVr",) + tuple(f"A{i}" for i in range(NCH)),
          writes=("PS4",), tag=f"tdmm{rb}")
        A("vector", lambda v: [v.tensor_copy(sTd[:, 0:RB], psf[4][0:4, 0:RB])],
          reads=("PS4",), writes=("sTd",), tag=f"tdev{rb}")

        def emit_stats2(oc):
            A("tensor", lambda t, oc=oc: [t.matmul(psf[3][0:1, 0:RB], cbuf["cOnesColb"][:, :],
                                                   sXq[oc % 4][:, 0:RB], start=(oc == 0),
                                                   stop=(oc == NCH - 1), skip_group_check=True)],
              reads=(f"XQ{oc % 4}", "c_cOnesColb"), writes=("PS3",), tag=f"sum2{rb}_{oc}")
            A("tensor", lambda t, oc=oc: [t.matmul(psf[5][0:1, 0:RB], cbuf["cOnesColb"][:, :],
                                                   sSq[oc % 4][:, 0:RB], start=(oc == 0),
                                                   stop=(oc == NCH - 1), skip_group_check=True)],
              reads=(f"SQ{oc % 4}", "c_cOnesColb"), writes=("PS5",), tag=f"sqm2{rb}_{oc}")
        for oc in range(NCH):
            slot = oc % 2
            A("sync", lambda s, oc=oc, slot=slot: [s.dma_start(
                out=sPan[slot][:], in_=bass.AP(UT, oc * D * 128, [[4096, 128], [1, 4096]]))],
              reads=tuple(f"UT{i}" for i in range(NCH)),
              writes=(f"pan{slot}",), ndma=1, tag=f"t2p{rb}_{oc}")
            pk = oc % 3
            def t2_mm(t, oc=oc, slot=slot, pk=pk):
                out = []
                for rc in range(NCH):
                    out.append(t.matmul(psf[pk][:, 0:RB],
                                        sPan[slot][:, rc * 128:(rc + 1) * 128],
                                        chB(rc), start=(rc == 0), stop=False))
                out.append(t.matmul(psf[pk][:, 0:RB], sQ[:, oc * 128:(oc + 1) * 128],
                                    sTd[0:4, 0:RB], start=False, stop=True))
                return out
            A("tensor", t2_mm,
              reads=(f"pan{slot}", "sQ", "sTd") + tuple(f"B{i}" for i in range(NCH)),
              writes=(PSK[pk],), tag=f"t2mm{rb}_{oc}")
            A("vector", lambda v, oc=oc, pk=pk: [v.scalar_tensor_tensor(
                sXq[oc % 4][:, 0:RB], psf[pk][:, 0:RB], sBcol[:, oc:oc + 1], chA(oc),
                OP.add, OP.add)],
              reads=(PSK[pk], "sBcol", f"A{oc}"), writes=(f"XQ{oc % 4}",), tag=f"t2ev{rb}_{oc}")
            A("scalar", lambda s, oc=oc, rb=rb: [s.dma_start(
                out=bass.AP(X2S, oc * 128 * rows + rb * RB, [[rows, 128], [1, RB]]),
                in_=sXq[oc % 4][:, 0:RB])],
              reads=(f"XQ{oc % 4}",), writes=(f"X2S{rb}_{oc}",), ndma=1, tag=f"x2st{rb}_{oc}")
            A("scalar", lambda s, oc=oc: [s.activation(sSq[oc % 4][:, 0:RB],
                                                       sXq[oc % 4][:, 0:RB], AF.Square)],
              reads=(f"XQ{oc % 4}",), writes=(f"SQ{oc % 4}",), tag=f"sq2{rb}_{oc}")
            if oc >= 3:
                emit_stats2(oc - 3)
        for oc in range(NCH - 3, NCH):
            emit_stats2(oc)
        if rb + 1 < n_rb:
            u_transposes(rb + 1)
        A("vector", lambda v: [v.tensor_scalar(mrow[:], psf[3][0:1, 0:RB], 1.0 / D, None, OP.mult)],
          reads=("PS3",), writes=("mrow",), tag=f"m2{rb}")
        A("vector", lambda v: [v.tensor_scalar(srow[:], psf[5][0:1, 0:RB], 1.0 / D, None, OP.mult)],
          reads=("PS5",), writes=("srow",), tag=f"s2{rb}")
        A("vector", lambda v: [v.tensor_tensor(rstdr[:], mrow[:], mrow[:], OP.mult)],
          reads=("mrow",), writes=("rstdr",))
        A("vector", lambda v: [v.scalar_tensor_tensor(srow[:], srow[:], 1e-5, rstdr[:],
                                                      OP.add, OP.subtract)],
          reads=("srow", "rstdr"), writes=("srow",))
        A("scalar", lambda s: [s.activation(srow[:], srow[:], AF.Ln)],
          reads=("srow",), writes=("srow",))
        A("scalar", lambda s: [s.activation(rstdr[:], srow[:], AF.Exp, scale=-0.5)],
          reads=("srow",), writes=("rstdr",))
        for g_ in range(NG):
            A("tensor", lambda t, g_=g_: [t.transpose(psf[6][0:128, 0:1],
                                                      mrow[0:1, g_ * 128:(g_ + 1) * 128],
                                                      cbuf["cIdf"][0:1, 0:1])],
              reads=("mrow", "c_cIdf"), writes=("PS6",))
            A("scalar", lambda s, g_=g_: [s.copy(m2col[:, g_:g_ + 1], psf[6][0:128, 0:1])],
              reads=("PS6",), writes=("m2col",))
            A("tensor", lambda t, g_=g_: [t.transpose(psf[6][0:128, 0:1],
                                                      rstdr[0:1, g_ * 128:(g_ + 1) * 128],
                                                      cbuf["cIdf"][0:1, 0:1])],
              reads=("rstdr", "c_cIdf"), writes=("PS6",))
            A("scalar", lambda s, g_=g_: [s.copy(r2col[:, g_:g_ + 1], psf[6][0:128, 0:1])],
              reads=("PS6",), writes=("r2col",))
        for g_ in range(NG):
            A("sync", lambda s, g_=g_, rb=rb, xs=g_ % 2: [s.dma_start(
                out=(sXin[:] if xs == 0 else sHB[:]),
                in_=bass.AP(X2S, rb * RB + g_ * 128, [[rows, D], [1, 128]]),
                transpose=True)],
              reads=tuple(f"X2S{rb}_{o}" for o in range(NCH)),
              writes=(("XIN" if g_ % 2 == 0 else "sHB"),), ndma=1, tag=f"xintr{rb}_{g_}")
            for hf in range(2):
                A("gpsimd", lambda p, g_=g_, hf=hf, xs=g_ % 2: [p.tensor_scalar(
                    sYout[hf][:, :],
                    (sXin[:] if xs == 0 else sHB[:])[:, hf * 2048:(hf + 1) * 2048],
                    m2col[:, g_:g_ + 1], r2col[:, g_:g_ + 1], OP.subtract, OP.mult)],
                  reads=(("XIN" if g_ % 2 == 0 else "sHB"), "m2col", "r2col"),
                  writes=(f"YO{hf}",), tag=f"ln2ap{rb}_{g_}_{hf}")
                A("sync", lambda s, g_=g_, hf=hf, rb=rb: [s.dma_start(
                    out=bass.AP(OUT, (rb * RB + g_ * 128) * D + hf * 2048, [[D, 128], [1, 2048]]),
                    in_=sYout[hf][:, :])],
                  reads=(f"YO{hf}",), writes=(f"OUT{rb}_{g_}_{hf}",), ndma=1, tag=f"outst{rb}_{g_}_{hf}")

    A("gpsimd", lambda g: [g.nop()],
      reads=tuple(f"OUT{rb}_{g_}_{hf}" for rb in range(n_rb)
                  for g_ in range(NG) for hf in range(2)), tag="final")

    P.schedule()
    sems = {}
    for s in ENG_SEM.values():
        sems[s] = es.enter_context(nc.semaphore(s))
    for q in Q_SEM.values():
        for ln in range(8):
            sems[q + str(ln)] = es.enter_context(nc.semaphore(q + str(ln)))

    by_eng = {}
    for o in P.ops:
        by_eng.setdefault(o["eng"], []).append(o)

    def run_ops(name, eng):
        for o in by_eng.get(name, ()):
            for (s, v) in o["waits"]:
                eng.wait_ge(sems[s], v)
            instrs = o["fn"](eng)
            if o["ndma"]:
                assert len(instrs) == o["ndma"], (o["tag"], len(instrs), o["ndma"])
                for ins in instrs:
                    ins.then_inc(sems[o["sem"]], 16)
            elif o["marked"]:
                instrs[-1].then_inc(sems[o["sem"]], 1)

    with nc.Block() as block:
        @block.tensor
        def _(t):
            run_ops("tensor", t)

        @block.vector
        def _(v):
            run_ops("vector", v)

        @block.scalar
        def _(s):
            run_ops("scalar", s)

        @block.gpsimd
        def _(g):
            run_ops("gpsimd", g)

        @block.sync
        def _(s):
            run_ops("sync", s)

    es.close()
    return nc


def make_inmaps(inputs, rows_per_core, ncore):
    u = np.ascontiguousarray(np.asarray(inputs["L0_attn_u"], np.float32)).reshape(-1, D)
    f32 = lambda x: np.ascontiguousarray(np.asarray(x, np.float32))
    shared = {
        "Lam": f32(inputs["L0_attn_Lambda_ri"]),
        "pq": np.concatenate([f32(inputs["L0_attn_p_ri"]).reshape(-1),
                              f32(inputs["L0_attn_q_ri"]).reshape(-1)]).reshape(1, 256),
        "Vc": f32(inputs["L0_attn_Vc_ri"]).reshape(64, 128),
        "Ct": f32(inputs["L0_attn_Ct_ri"]).reshape(1, 128),
        "Bp": f32(inputs["L0_attn_Bp"]).reshape(64, 1),
        "aD": f32(inputs["L0_attn_D"]).reshape(1, 1),
        "lstep": f32(inputs["L0_attn_log_step"]).reshape(1, 1),
        "Utop": f32(inputs["L0_ffn_U_top"]),
        "Stop": f32(inputs["L0_ffn_S_top"]).reshape(1, 64),
        "Vtop": f32(inputs["L0_ffn_Vh_top"]),
        "Utail": f32(inputs["L0_ffn_U_tail"]),
        "Stail": f32(inputs["L0_ffn_S_tail"]).reshape(-1),
        "Vtail": f32(inputs["L0_ffn_Vh_tail"]),
        "Ur": f32(inputs["L0_ffn_U_tail_r"]),
        "Str": f32(inputs["L0_ffn_S_tail_r"]).reshape(1, 4),
        "Vr4": f32(inputs["L0_ffn_Vh_tail_r"]),
        "alpha": f32(inputs["L0_ffn_alpha"]).reshape(1, 64),
        "beta": f32(inputs["L0_ffn_beta"]).reshape(1, 64),
        "fD": f32(inputs["L0_ffn_D"]).reshape(1, 4),
        "Rm": f32(inputs["L0_ffn_R"]),
        "bias": f32(inputs["L0_ffn_bias"]).reshape(-1),
    }
    for n, (k, a) in _consts().items():
        shared[n] = a
    maps = []
    for c in range(ncore):
        m = dict(shared)
        m["u"] = u[c * rows_per_core:(c + 1) * rows_per_core]
        maps.append(m)
    return maps


def kernel(**inputs):
    rows = 2048
    nc = build(rows)
    in_maps = make_inmaps(inputs, rows, NCORE)
    res = run_bass_kernel_spmd(nc, in_maps, core_ids=list(range(NCORE)))
    out = np.concatenate([np.asarray(r["OUT"], np.float32) for r in res.results], axis=0)
    return out.reshape(8, 2048, D)


# version 11
# speedup vs baseline: 1.2253x; 1.2253x over previous
import numpy as np
import ml_dtypes
from contextlib import ExitStack
import concourse.bass as bass
import concourse.mybir as mybir
from concourse.bass_utils import run_bass_kernel_spmd

F32 = mybir.dt.float32
BF16 = mybir.dt.bfloat16
AF = mybir.ActivationFunctionType
OP = mybir.AluOpType
bf16 = ml_dtypes.bfloat16

D = 4096
NCH = 32
NCORE = 8

ENG_SEM = {"tensor": "sPE", "vector": "sDV", "scalar": "sAC",
           "gpsimd": "sGP", "sync": "sSY"}
Q_SEM = {"gpsimd": "qGP", "sync": "qSY", "scalar": "qAC"}


def _consts():
    c = {}
    jj = np.arange(D, dtype=np.float64)
    Om = np.exp(-2j * np.pi * jj / D)
    T0 = (1.0 - Om) / (1.0 + Om)
    cc = 2.0 / (1.0 + Om)
    c["cT0"] = ("f", np.concatenate([T0.real, T0.imag]).astype(np.float32).reshape(1, 2 * D))
    ccm = cc.reshape(NCH, 128).T
    c["cCre"] = ("f", ccm.real.astype(np.float32).copy())
    c["cCim"] = ("f", ccm.imag.astype(np.float32).copy())
    W32 = np.exp(2j * np.pi * np.outer(np.arange(32), np.arange(32)) / 32)
    c["cAre"] = ("f", W32.real.astype(np.float32).copy())
    c["cAim"] = ("f", W32.imag.astype(np.float32).copy())
    c["cAimN"] = ("f", (-W32.imag).astype(np.float32).copy())
    T2w = np.exp(2j * np.pi * np.outer(np.arange(128), np.arange(32)) / D)
    c["cT2re"] = ("f", T2w.real.astype(np.float32).copy())
    c["cT2im"] = ("f", T2w.imag.astype(np.float32).copy())
    W128 = np.exp(2j * np.pi * np.outer(np.arange(128), np.arange(128)) / 128) / D
    c["cBre"] = ("f", W128.real.astype(np.float32).copy())
    c["cBimN"] = ("f", (-W128.imag).astype(np.float32).copy())
    c["cIdf"] = ("f", np.eye(128, dtype=np.float32))
    c["cIdb"] = ("b", np.eye(128, dtype=np.float32).astype(bf16))
    c["cJb"] = ("b", np.eye(128, dtype=np.float32)[::-1].copy().astype(bf16))
    c["cOnesf"] = ("f", np.ones((1, 128), dtype=np.float32))
    c["cOnesb"] = ("b", np.ones((1, 128), dtype=bf16))
    c["cNOnesb"] = ("b", np.full((1, 128), -1.0, np.float32).astype(bf16))
    c["cOnesColb"] = ("b", np.ones((128, 1), dtype=bf16))
    return c


class Prog:

    def __init__(self):
        self.ops = []

    def add(self, eng, fn, reads=(), writes=(), ndma=0, tag=""):
        self.ops.append(dict(eng=eng, fn=fn, reads=tuple(reads),
                             writes=tuple(writes), ndma=ndma, tag=tag))

    def schedule(self):
        writers, readers = {}, {}
        for i, o in enumerate(self.ops):
            deps = set()
            for k in o["reads"]:
                deps.update(writers.get(k, ()))
            for k in o["writes"]:
                deps.update(writers.get(k, ()))
                deps.update(readers.get(k, ()))
            deps.discard(i)
            if o["eng"] == "tensor" and not o["ndma"]:
                deps = {d for d in deps
                        if not (self.ops[d]["eng"] == "tensor" and not self.ops[d]["ndma"])}
            o["deps"] = deps
            for k in o["reads"]:
                readers.setdefault(k, []).append(i)
            for k in o["writes"]:
                writers[k] = [i]
                readers[k] = []
        NL = 8
        rrc = {}
        last_on_lane = {}
        for i, o in enumerate(self.ops):
            if o["ndma"]:
                q = Q_SEM[o["eng"]]
                lane = rrc.get(q, 0) % NL
                rrc[q] = rrc.get(q, 0) + 1
                o["sem"] = q + str(lane)
                prev = last_on_lane.get(o["sem"])
                if prev is not None:
                    o["deps"].add(prev)
                last_on_lane[o["sem"]] = i
            else:
                o["sem"] = ENG_SEM[o["eng"]]
        needed = set()
        for o in self.ops:
            needed.update(o["deps"])
        cnt = {}
        for i, o in enumerate(self.ops):
            o["marked"] = (i in needed) or bool(o["ndma"])
            if o["marked"]:
                cnt[o["sem"]] = cnt.get(o["sem"], 0) + (16 * o["ndma"] if o["ndma"] else 1)
                o["val"] = cnt[o["sem"]]
        wm = {}
        for i, o in enumerate(self.ops):
            ws = {}
            for d in o["deps"]:
                p = self.ops[d]
                assert p["marked"], (i, d, p["tag"])
                s = p["sem"]
                ws[s] = max(ws.get(s, 0), p["val"])
            final = []
            for s, v in sorted(ws.items()):
                key = (o["eng"], s)
                if wm.get(key, 0) < v:
                    wm[key] = v
                    final.append((s, v))
            o["waits"] = final


def build(rows):
    RB = min(512, rows)
    n_rb = rows // RB
    NG = RB // 128
    CONSTS = _consts()

    nc = bass.Bass("TRN2", target_bir_lowering=False)
    P = Prog()
    es = ExitStack()

    IN = {}
    def din(name, shape, dtype=F32):
        IN[name] = nc.dram_tensor(name, list(shape), dtype, kind="ExternalInput")
    din("u", (rows, D))
    din("Lam", (64, 2)); din("pq", (1, 256)); din("Vc", (64, 128))
    din("Ct", (1, 128)); din("Bp", (64, 1)); din("aD", (1, 1)); din("lstep", (1, 1))
    din("Utop", (D, 64)); din("Stop", (1, 64)); din("Vtop", (64, D))
    din("Utail", (D, 4032)); din("Stail", (4032,)); din("Vtail", (4032, D))
    din("Ur", (D, 4)); din("Str", (1, 4)); din("Vr4", (4, D))
    din("alpha", (1, 64)); din("beta", (1, 64)); din("fD", (1, 4))
    din("Rm", (4, 4)); din("bias", (4096,))
    for n, (k, a) in CONSTS.items():
        din(n, a.shape, F32 if k == "f" else BF16)

    OUT = nc.dram_tensor("OUT", [rows, D], F32, kind="ExternalOutput")
    u_tm = nc.dram_tensor("u_tm", [rows * D], BF16)
    VhPad = nc.dram_tensor("VhPad", [D * D], BF16)
    UPad = nc.dram_tensor("UPad", [D * D], BF16)
    VT = nc.dram_tensor("VT", [NCH * D * 128], BF16)
    UT = nc.dram_tensor("UT", [NCH * D * 128], BF16)
    X2S = nc.dram_tensor("X2S", [D * rows], BF16)
    Kpad = nc.dram_tensor("Kpad", [4224], F32)

    def sb(name, p, f, dtype):
        return es.enter_context(nc.sbuf_tensor(name, [p, f], dtype))
    def psum(name, p, f, dtype=F32):
        return es.enter_context(nc.psum_tensor(name, [p, f], dtype))

    sA = sb("sA", 128, NCH * RB, BF16)
    sB = sb("sB", 128, NCH * RB, BF16)
    sG = sb("sG", 128, NCH * 128, BF16)
    sPan = [sb(f"sPan{j}", 128, D, BF16) for j in range(2)]
    sXq = [sb(f"sXq{j}", 128, RB, BF16) for j in range(4)]
    sSq = [sb(f"sSq{j}", 128, RB, BF16) for j in range(4)]
    sTmp = [sb(f"sTmp{j}", 128, RB, BF16) for j in range(2)]
    sRstd = sb("sRstd", 128, RB, BF16)
    sAmn = sb("sAmn", 128, RB, BF16)
    sXin = sb("sXin", 128, D, BF16)
    sYout = [sb(f"sYout{j}", 128, 2048, F32) for j in range(2)]
    sHF = sb("sHF", 128, 2048, F32)
    sHB = sb("sHB", 128, 2048, BF16)
    sVr4b = sb("sVr4b", 4, D, BF16)
    sUr32f = sb("sUr32f", 128, 128, F32)
    sUr32b = sb("sUr32b", 128, 128, BF16)
    kv = sb("kv", 128, 512, F32)
    wood = sb("wood", 128, 32 * 12, F32)
    sQ = sb("sQ", 4, D, BF16)
    sTd = sb("sTd", 4, RB, BF16)
    sVr = sb("sVr", 128, NCH * 4, BF16)
    sScol = sb("sScol", 128, NCH, F32)
    sBcol = sb("sBcol", 128, NCH, F32)
    sRows32 = sb("sRows32", 32, 128, F32)
    m2col = sb("m2col", 128, NG, F32)
    r2col = sb("r2col", 128, NG, F32)
    mrow = sb("mrow", 1, RB, F32)
    srow = sb("srow", 1, RB, F32)
    rstdr = sb("rstdr", 1, RB, F32)
    rstd_b = sb("rstd_b", 1, RB, BF16)
    amn_b = sb("amn_b", 1, RB, BF16)
    zrow = sb("zrow", 1, 128, F32)
    Dp1 = sb("Dp1", 128, 1, F32)
    vcraw = sb("vcraw", 64, 128, F32)
    vcre = sb("vcre", 64, 64, F32); vcim = sb("vcim", 64, 64, F32)
    vctre = sb("vctre", 64, 64, F32); vctim = sb("vctim", 64, 64, F32)
    bp = sb("bp", 64, 1, F32)
    brow = sb("brow", 1, 128, F32)
    ctraw = sb("ctraw", 1, 128, F32)
    pqraw = sb("pqraw", 1, 256, F32)
    vv8 = sb("vv8", 1, 512, F32)
    vscr = sb("vscr", 1, 64, F32)
    vvt = sb("vvt", 64, 8, F32)
    lam2 = sb("lam2", 64, 2, F32)
    nlam = sb("nlam", 64, 2, F32)
    lsr = sb("lsr", 1, 1, F32)
    s2r = sb("s2r", 1, 64, F32)
    T0q = sb("T0q", 1, 2048, F32)
    denRe = sb("denRe", 64, 1024, F32)
    denIm = sb("denIm", 64, 1024, F32)
    den2 = sb("den2", 64, 1024, F32)
    dscr = sb("dscr", 64, 1024, F32)
    arre = sb("arre", 128, 32, F32); arim = sb("arim", 128, 32, F32)
    arTre = sb("arTre", 32, 128, F32); arTim = sb("arTim", 32, 128, F32)
    aare = sb("aare", 128, 32, F32); aaim = sb("aaim", 128, 32, F32)
    a2re = sb("a2re", 128, 32, F32); a2im = sb("a2im", 128, 32, F32)
    sKm = sb("sKm", 128, 32, F32)
    sal = sb("sal", 1, 64, F32); sbe = sb("sbe", 1, 64, F32)
    shead = sb("shead", 1, 64, F32)
    fd4 = sb("fd4", 4, 1, F32); sd4 = sb("sd4", 4, 1, F32)
    sRm = sb("sRm", 4, 4, F32); sRsb = sb("sRsb", 4, 4, BF16)
    ur_t = sb("ur_t", 4, 128, BF16)

    cbuf = {}
    for n, (k, a) in CONSTS.items():
        if n == "cT0":
            continue
        cbuf[n] = sb("c_" + n, a.shape[0], a.shape[1], F32 if k == "f" else BF16)

    psf = [psum(f"psf{i}", 128, 512, F32) for i in range(7)]
    psb_t = psum("psb_t", 128, 1024, BF16)
    PSK = [f"PS{i}" for i in range(7)]

    A = P.add

    def chA(ic):
        return sA[:, ic * RB:(ic + 1) * RB]
    def chB(rc):
        return sB[:, rc * RB:(rc + 1) * RB]

    def ld1(dst, src, key, tag=""):
        A("gpsimd", lambda g, d=dst, s=src: [g.dma_start(out=d, in_=s)],
          writes=(key,), ndma=1, tag=tag or ("ld-" + key))

    ld1(lam2[:], IN["Lam"][:, :], "lam2")
    ld1(pqraw[:], IN["pq"][:, :], "pqraw")
    ld1(vcraw[:], IN["Vc"][:, :], "vcraw")
    ld1(ctraw[:], IN["Ct"][:, :], "ctraw")
    ld1(bp[:], IN["Bp"][:, :], "bp")
    ld1(lsr[:], IN["lstep"][:, :], "lsr")
    ld1(sal[:], IN["alpha"][:, :], "sal")
    ld1(sbe[:], IN["beta"][:, :], "sbe")
    ld1(shead[:], IN["Stop"][:, :], "shead")
    ld1(sRm[:], IN["Rm"][:, :], "sRm")
    ld1(fd4[:], bass.AP(IN["fD"], 0, [[1, 4], [1, 1]]), "fd4")
    ld1(sd4[:], bass.AP(IN["Str"], 0, [[1, 4], [1, 1]]), "sd4")
    ld1(Dp1[0:1, 0:1], IN["aD"][:, :], "Dp1")
    for n in CONSTS:
        if n == "cT0":
            continue
        ld1(cbuf[n][:], IN[n][:, :], "c_" + n)

    nq = 4 if rows % 4 == 0 and rows >= 512 else 1
    RQ = rows // nq
    for k in range(nq):
        A("gpsimd", lambda g, k=k: [g.dma_start(
            out=bass.AP(u_tm, k * RQ * D, [[D, RQ], [1, D]]),
            in_=IN["u"][k * RQ:(k + 1) * RQ, :])],
          writes=(f"u_tm{k}",), ndma=1, tag=f"castu{k}")

    A("gpsimd", lambda g: [g.dma_start(out=sVr4b[:], in_=IN["Vr4"][:, :])],
      writes=("sVr4b",), ndma=1, tag="vr4b")
    A("scalar", lambda s: [
        s.dma_start(out=bass.AP(sUr32f, q * 32, [[128, 128], [4, 8], [1, 4]]),
                    in_=bass.AP(IN["Ur"], q * 8 * 512, [[4, 128], [512, 8], [1, 4]]))
        for q in range(4)],
      writes=("sUr32f",), ndma=4, tag="ur32f")
    A("vector", lambda v: [v.tensor_copy(sUr32b[:], sUr32f[:])],
      reads=("sUr32f",), writes=("sUr32b",), tag="ur32cast")

    for k in range(4):
        c0, c1 = 1024 * k, 1024 * (k + 1)
        A("gpsimd", lambda g, c0=c0, c1=c1: [
            g.dma_start(out=bass.AP(VhPad, c0, [[D, 4032], [1, 1024]]),
                        in_=bass.AP(IN["Vtail"], c0, [[D, 4032], [1, 1024]])),
            g.dma_start(out=bass.AP(VhPad, 4032 * D + c0, [[D, 64], [1, 1024]]),
                        in_=bass.AP(IN["Vtop"], c0, [[D, 64], [1, 1024]])),
        ], writes=(f"VhPad{k}",), ndma=2, tag=f"castVh{k}")
    for k in range(4):
        c0 = 1024 * k
        tw = 1024 if k < 3 else 4032 - 3072
        def _ucast(g, c0=c0, tw=tw, k=k):
            out = [g.dma_start(out=bass.AP(UPad, c0, [[D, D], [1, tw]]),
                               in_=bass.AP(IN["Utail"], c0, [[4032, D], [1, tw]]))]
            if k == 3:
                out.append(g.dma_start(out=bass.AP(UPad, 4032, [[D, D], [1, 64]]),
                                       in_=bass.AP(IN["Utop"], 0, [[64, D], [1, 64]])))
            return out
        A("gpsimd", _ucast, writes=(f"UPad{k}",), ndma=(2 if k == 3 else 1),
          tag=f"castU{k}")

    def u_transposes(rb):
        q0 = (rb * RB) // RQ
        q1 = ((rb + 1) * RB - 1) // RQ
        ukeys = tuple(f"u_tm{k}" for k in range(q0, q1 + 1))
        for ic in range(NCH):
            A("sync", lambda s, ic=ic, rb=rb: [s.dma_start(
                out=chA(ic),
                in_=bass.AP(u_tm, rb * RB * D + ic * 128, [[D, RB], [1, 128]]),
                transpose=True)],
              reads=ukeys, writes=(f"A{ic}",), ndma=1, tag=f"utr{rb}_{ic}")
    u_transposes(0)

    def tiled_prepass(SrcPad, Dst, dkey, tag, slot_aps, slot_keys):
        pk = "VhPad" if SrcPad is VhPad else "UPad"
        def tr(ic):
            slot = ic % 2
            A("sync", lambda s, ic=ic, slot=slot: [s.dma_start(
                out=slot_aps[slot],
                in_=bass.AP(SrcPad, ic * 128, [[D, D], [1, 128]]), transpose=True)],
              reads=(pk + str(ic // 8),),
              writes=(slot_keys[slot],), ndma=1, tag=f"{tag}tr{ic}")
        def st(ic):
            slot = ic % 2
            A("sync", lambda s, ic=ic, slot=slot: [
                s.dma_start(
                    out=bass.AP(Dst, g8 * 8 * D * 128 + ic * 128,
                                [[4096, 128], [D * 128, 8], [1, 128]]),
                    in_=bass.AP(slot_aps[slot].tensor, slot_aps[slot].offset + g8 * 1024,
                                [[D, 128], [128, 8], [1, 128]]))
                for g8 in range(4)],
              reads=(slot_keys[slot],), writes=(dkey + str(ic),), ndma=4, tag=f"{tag}st{ic}")
        for ic in range(0, NCH, 2):
            tr(ic); tr(ic + 1); st(ic); st(ic + 1)

    tiled_prepass(VhPad, VT, "VT", "vt", [sPan[0][:], sPan[1][:]], ["pan0", "pan1"])
    tiled_prepass(UPad, UT, "UT", "ut",
                  [sXin[:], sYout[0][:].bitcast(BF16)], ["XIN", "YO0"])

    A("vector", lambda v: [v.tensor_scalar(Dp1[0:1, 0:1], Dp1[0:1, 0:1], 1.0, None, OP.add)],
      reads=("Dp1",), writes=("Dp1",))
    A("tensor", lambda t: [t.matmul(psf[6][0:128, 0:1], cbuf["cOnesf"][:, :], Dp1[0:1, 0:1],
                                    start=True, stop=True)],
      reads=("Dp1", "c_cOnesf"), writes=("PS6",))
    A("scalar", lambda s: [s.copy(Dp1[:, 0:1], psf[6][0:128, 0:1])],
      reads=("PS6",), writes=("Dp1",))

    A("vector", lambda v: [v.tensor_copy(vcre[:], bass.AP(vcraw, 0, [[128, 64], [2, 64]]))],
      reads=("vcraw",), writes=("vcre",))
    A("vector", lambda v: [v.tensor_copy(vcim[:], bass.AP(vcraw, 1, [[128, 64], [2, 64]]))],
      reads=("vcraw",), writes=("vcim",))
    A("tensor", lambda t: [t.transpose(psf[4][0:64, 0:64], vcre[:], cbuf["cIdf"][0:64, 0:64])],
      reads=("vcre", "c_cIdf"), writes=("PS4",))
    A("vector", lambda v: [v.tensor_copy(vctre[:], psf[4][0:64, 0:64])],
      reads=("PS4",), writes=("vctre",))
    A("tensor", lambda t: [t.transpose(psf[4][0:64, 0:64], vcim[:], cbuf["cIdf"][0:64, 0:64])],
      reads=("vcim", "c_cIdf"), writes=("PS4",))
    A("vector", lambda v: [v.tensor_copy(vctim[:], psf[4][0:64, 0:64])],
      reads=("PS4",), writes=("vctim",))
    A("tensor", lambda t: [t.matmul(psf[5][0:1, 0:64], bp[:], vctre[:], start=True, stop=True)],
      reads=("bp", "vctre"), writes=("PS5",))
    A("vector", lambda v: [v.tensor_copy(brow[0:1, 0:64], psf[5][0:1, 0:64])],
      reads=("PS5",), writes=("brow",))
    A("tensor", lambda t: [t.matmul(psf[5][0:1, 0:64], bp[:], vctim[:], start=True, stop=True)],
      reads=("bp", "vctim"), writes=("PS5",))
    A("vector", lambda v: [v.tensor_copy(brow[0:1, 64:128], psf[5][0:1, 0:64])],
      reads=("PS5",), writes=("brow",))

    A("vector", lambda v: [v.tensor_tensor(shead[:], shead[:], sal[:], OP.mult)],
      reads=("shead", "sal"), writes=("shead",))
    A("vector", lambda v: [v.tensor_tensor(shead[:], shead[:], sbe[:], OP.add)],
      reads=("shead", "sbe"), writes=("shead",))
    A("vector", lambda v: [v.tensor_relu(shead[:], shead[:])],
      reads=("shead",), writes=("shead",))

    ctre = bass.AP(ctraw, 0, [[128, 1], [2, 64]])
    ctim = bass.AP(ctraw, 1, [[128, 1], [2, 64]])
    pre_ = bass.AP(pqraw, 0, [[256, 1], [2, 64]])
    pim_ = bass.AP(pqraw, 1, [[256, 1], [2, 64]])
    qre_ = bass.AP(pqraw, 128, [[256, 1], [2, 64]])
    qim_ = bass.AP(pqraw, 129, [[256, 1], [2, 64]])
    bre_ = brow[0:1, 0:64]
    bim_ = brow[0:1, 64:128]

    def vprod(k, are, aim, xre, xim):
        o_re = vv8[0:1, k * 64:(k + 1) * 64]
        o_im = vv8[0:1, (k + 1) * 64:(k + 2) * 64]
        src = ("ctraw", "pqraw", "brow")
        A("vector", lambda v: [v.tensor_tensor(o_re, are, xre, OP.mult)],
          reads=src, writes=(f"vv{k}",))
        A("vector", lambda v: [v.tensor_tensor(vscr[:], aim, xim, OP.mult)],
          reads=src, writes=("vscr",))
        A("vector", lambda v: [v.tensor_tensor(o_re, o_re, vscr[:], OP.add)],
          reads=(f"vv{k}", "vscr"), writes=(f"vv{k}",))
        A("vector", lambda v: [v.tensor_tensor(o_im, are, xim, OP.mult)],
          reads=src, writes=(f"vv{k + 1}",))
        A("vector", lambda v: [v.tensor_tensor(vscr[:], aim, xre, OP.mult)],
          reads=src, writes=("vscr",))
        A("vector", lambda v: [v.tensor_tensor(o_im, o_im, vscr[:], OP.subtract)],
          reads=(f"vv{k + 1}", "vscr"), writes=(f"vv{k + 1}",))

    vprod(0, ctre, ctim, bre_, bim_)
    vprod(2, ctre, ctim, pre_, pim_)
    vprod(4, qre_, qim_, bre_, bim_)
    vprod(6, qre_, qim_, pre_, pim_)

    for k in range(8):
        A("scalar", lambda s, k=k: [s.dma_start(out=sRows32[k:k + 1, 0:64],
                                                in_=vv8[0:1, k * 64:(k + 1) * 64])],
          reads=(f"vv{k}",), writes=("sR32",), ndma=1, tag=f"vvasm{k}")
    A("tensor", lambda t: [t.transpose(psf[5][0:64, 0:8], sRows32[0:8, 0:64], cbuf["cIdf"][0:8, 0:8])],
      reads=("sR32", "c_cIdf"), writes=("PS5",))
    A("vector", lambda v: [v.tensor_copy(vvt[:], psf[5][0:64, 0:8])],
      reads=("PS5",), writes=("vvt",))

    A("scalar", lambda s: [s.activation(lsr[:], lsr[:], AF.Exp)],
      reads=("lsr",), writes=("lsr",))
    A("vector", lambda v: [v.reciprocal(lsr[:], lsr[:])],
      reads=("lsr",), writes=("lsr",))
    A("vector", lambda v: [v.tensor_scalar(lsr[:], lsr[:], 2.0, None, OP.mult)],
      reads=("lsr",), writes=("lsr",))
    A("vector", lambda v: [v.tensor_scalar(s2r[:], cbuf["cOnesf"][0:1, 0:64], lsr[:], None, OP.mult)],
      reads=("c_cOnesf", "lsr"), writes=("s2r",))
    A("vector", lambda v: [v.tensor_scalar(nlam[:], lam2[:], -1.0, None, OP.mult)],
      reads=("lam2",), writes=("nlam",))

    for q in range(4):
        A("scalar", lambda s, q=q: [
            s.dma_start(out=T0q[0:1, 0:1024],
                        in_=bass.AP(IN["cT0"], q * 1024, [[2 * D, 1], [1, 1024]])),
            s.dma_start(out=T0q[0:1, 1024:2048],
                        in_=bass.AP(IN["cT0"], D + q * 1024, [[2 * D, 1], [1, 1024]])),
        ], writes=("T0q",), ndma=2, tag=f"T0q{q}")
        for h2 in range(2):
            sl = slice(h2 * 512, (h2 + 1) * 512)
            sl2 = slice(1024 + h2 * 512, 1024 + (h2 + 1) * 512)
            A("tensor", lambda t, sl=sl: [t.matmul(psf[0][0:64, 0:512], s2r[0:1, :],
                                                   T0q[0:1, sl], start=True, stop=True)],
              reads=("s2r", "T0q"), writes=("PS0",))
            A("scalar", lambda s, sl=sl: [s.activation(denRe[:, sl], psf[0][0:64, 0:512],
                                                       AF.Identity, bias=nlam[:, 0:1])],
              reads=("PS0", "nlam"), writes=("denRe",))
            A("tensor", lambda t, sl2=sl2: [t.matmul(psf[1][0:64, 0:512], s2r[0:1, :],
                                                     T0q[0:1, sl2], start=True, stop=True)],
              reads=("s2r", "T0q"), writes=("PS1",))
            A("scalar", lambda s, sl=sl: [s.activation(denIm[:, sl], psf[1][0:64, 0:512],
                                                       AF.Identity, bias=nlam[:, 1:2])],
              reads=("PS1", "nlam"), writes=("denIm",))
        A("vector", lambda v: [v.tensor_tensor(den2[:], denRe[:], denRe[:], OP.mult)],
          reads=("denRe",), writes=("den2",))
        A("vector", lambda v: [v.tensor_tensor(dscr[:], denIm[:], denIm[:], OP.mult)],
          reads=("denIm",), writes=("dscr",))
        A("vector", lambda v: [v.tensor_tensor(den2[:], den2[:], dscr[:], OP.add)],
          reads=("den2", "dscr"), writes=("den2",))
        A("vector", lambda v: [v.reciprocal(den2[:], den2[:])],
          reads=("den2",), writes=("den2",))
        A("vector", lambda v: [v.tensor_tensor(denRe[:], denRe[:], den2[:], OP.mult)],
          reads=("denRe", "den2"), writes=("denRe",))
        A("vector", lambda v: [v.scalar_tensor_tensor(denIm[:], denIm[:], -1.0, den2[:],
                                                      OP.mult, OP.mult)],
          reads=("denIm", "den2"), writes=("denIm",))
        pk = 2 + (q % 2)
        def cau_mm(t, q=q, pk=pk):
            out = []
            for tl in range(8):
                out.append(t.matmul(psf[pk][0:128, tl * 16:tl * 16 + 8],
                                    denRe[:, tl * 128:(tl + 1) * 128],
                                    vvt[:], start=True, stop=True))
                out.append(t.matmul(psf[pk][0:128, tl * 16 + 8:tl * 16 + 16],
                                    denIm[:, tl * 128:(tl + 1) * 128],
                                    vvt[:], start=True, stop=True))
            return out
        A("tensor", cau_mm, reads=("denRe", "denIm", "vvt"), writes=(PSK[pk],), tag=f"cau{q}")
        A("vector", lambda v, q=q, pk=pk: [
            v.tensor_copy(kv[:, q * 128:(q + 1) * 128], psf[pk][0:128, 0:128])],
          reads=(PSK[pk],), writes=("kv",), tag=f"cauev{q}")

    def Am(m):
        return bass.AP(kv, m, [[512, 128], [16, 32]])
    def Bm(m):
        return bass.AP(kv, 8 + m, [[512, 128], [16, 32]])
    def W(k):
        return wood[:, 32 * k:32 * (k + 1)]

    def dve(fn, reads, writes, tag=""):
        A("vector", fn, reads=reads, writes=writes, tag=tag)

    dve(lambda v: [v.tensor_tensor(W(0), Am(0), Bm(1), OP.subtract)], ("kv",), ("wood",))
    dve(lambda v: [v.tensor_tensor(W(1), Bm(0), Am(1), OP.add)], ("kv",), ("wood",))
    dve(lambda v: [v.tensor_tensor(W(2), Am(2), Bm(3), OP.subtract)], ("kv",), ("wood",))
    dve(lambda v: [v.tensor_tensor(W(3), Bm(2), Am(3), OP.add)], ("kv",), ("wood",))
    dve(lambda v: [v.tensor_tensor(W(4), Am(4), Bm(5), OP.subtract)], ("kv",), ("wood",))
    dve(lambda v: [v.tensor_tensor(W(5), Bm(4), Am(5), OP.add)], ("kv",), ("wood",))
    dve(lambda v: [v.tensor_tensor(W(6), Am(6), Bm(7), OP.subtract)], ("kv",), ("wood",))
    dve(lambda v: [v.tensor_tensor(W(7), Bm(6), Am(7), OP.add)], ("kv",), ("wood",))
    dve(lambda v: [v.tensor_tensor(W(8), W(2), W(4), OP.mult)], ("wood",), ("wood",))
    dve(lambda v: [v.tensor_tensor(W(9), W(3), W(5), OP.mult)], ("wood",), ("wood",))
    dve(lambda v: [v.tensor_tensor(W(8), W(8), W(9), OP.subtract)], ("wood",), ("wood",))
    dve(lambda v: [v.tensor_tensor(W(9), W(2), W(5), OP.mult)], ("wood",), ("wood",))
    dve(lambda v: [v.tensor_tensor(W(10), W(3), W(4), OP.mult)], ("wood",), ("wood",))
    dve(lambda v: [v.tensor_tensor(W(9), W(9), W(10), OP.add)], ("wood",), ("wood",))
    dve(lambda v: [v.tensor_scalar(W(6), W(6), 1.0, None, OP.add)], ("wood",), ("wood",))
    dve(lambda v: [v.tensor_tensor(W(10), W(6), W(6), OP.mult)], ("wood",), ("wood",))
    dve(lambda v: [v.tensor_tensor(W(11), W(7), W(7), OP.mult)], ("wood",), ("wood",))
    dve(lambda v: [v.tensor_tensor(W(10), W(10), W(11), OP.add)], ("wood",), ("wood",))
    dve(lambda v: [v.reciprocal(W(10), W(10))], ("wood",), ("wood",))
    dve(lambda v: [v.tensor_tensor(W(11), W(8), W(6), OP.mult)], ("wood",), ("wood",))
    dve(lambda v: [v.tensor_tensor(W(2), W(9), W(7), OP.mult)], ("wood",), ("wood",))
    dve(lambda v: [v.tensor_tensor(W(11), W(11), W(2), OP.add)], ("wood",), ("wood",))
    dve(lambda v: [v.tensor_tensor(W(11), W(11), W(10), OP.mult)], ("wood",), ("wood",))
    dve(lambda v: [v.tensor_tensor(W(3), W(9), W(6), OP.mult)], ("wood",), ("wood",))
    dve(lambda v: [v.tensor_tensor(W(4), W(8), W(7), OP.mult)], ("wood",), ("wood",))
    dve(lambda v: [v.tensor_tensor(W(3), W(3), W(4), OP.subtract)], ("wood",), ("wood",))
    dve(lambda v: [v.tensor_tensor(W(3), W(3), W(10), OP.mult)], ("wood",), ("wood",))
    dve(lambda v: [v.tensor_tensor(W(0), W(0), W(11), OP.subtract)], ("wood",), ("wood",))
    dve(lambda v: [v.tensor_tensor(W(1), W(1), W(3), OP.subtract)], ("wood",), ("wood",))
    dve(lambda v: [v.tensor_tensor(arre[:], cbuf["cCre"][:], W(0), OP.mult)],
        ("wood", "c_cCre"), ("arre",))
    dve(lambda v: [v.tensor_tensor(W(4), cbuf["cCim"][:], W(1), OP.mult)], ("wood", "c_cCim"), ("wood",))
    dve(lambda v: [v.tensor_tensor(arre[:], arre[:], W(4), OP.subtract)], ("arre", "wood"), ("arre",))
    dve(lambda v: [v.tensor_tensor(arim[:], cbuf["cCre"][:], W(1), OP.mult)],
        ("wood", "c_cCre"), ("arim",))
    dve(lambda v: [v.tensor_tensor(W(5), cbuf["cCim"][:], W(0), OP.mult)], ("wood", "c_cCim"), ("wood",))
    dve(lambda v: [v.tensor_tensor(arim[:], arim[:], W(5), OP.add)], ("arim", "wood"), ("arim",))

    A("tensor", lambda t: [t.transpose(psf[4][0:32, 0:128], arre[:], cbuf["cIdf"][:, :])],
      reads=("arre", "c_cIdf"), writes=("PS4",))
    A("vector", lambda v: [v.tensor_copy(arTre[:], psf[4][0:32, 0:128])],
      reads=("PS4",), writes=("arTre",))
    A("tensor", lambda t: [t.transpose(psf[4][0:32, 0:128], arim[:], cbuf["cIdf"][:, :])],
      reads=("arim", "c_cIdf"), writes=("PS4",))
    A("vector", lambda v: [v.tensor_copy(arTim[:], psf[4][0:32, 0:128])],
      reads=("PS4",), writes=("arTim",))
    A("tensor", lambda t: [
        t.matmul(psf[0][0:128, 0:32], arTre[:], cbuf["cAre"][:, :], start=True, stop=False),
        t.matmul(psf[0][0:128, 0:32], arTim[:], cbuf["cAimN"][:, :], start=False, stop=True)],
      reads=("arTre", "arTim", "c_cAre", "c_cAimN"), writes=("PS0",))
    A("vector", lambda v: [v.tensor_copy(aare[:], psf[0][0:128, 0:32])],
      reads=("PS0",), writes=("aare",))
    A("tensor", lambda t: [
        t.matmul(psf[1][0:128, 0:32], arTre[:], cbuf["cAim"][:, :], start=True, stop=False),
        t.matmul(psf[1][0:128, 0:32], arTim[:], cbuf["cAre"][:, :], start=False, stop=True)],
      reads=("arTre", "arTim", "c_cAre", "c_cAim"), writes=("PS1",))
    A("vector", lambda v: [v.tensor_copy(aaim[:], psf[1][0:128, 0:32])],
      reads=("PS1",), writes=("aaim",))
    dve(lambda v: [v.tensor_tensor(a2re[:], aare[:], cbuf["cT2re"][:, :], OP.mult)],
        ("aare", "c_cT2re"), ("a2re",))
    dve(lambda v: [v.tensor_tensor(a2im[:], aaim[:], cbuf["cT2im"][:, :], OP.mult)],
        ("aaim", "c_cT2im"), ("a2im",))
    dve(lambda v: [v.tensor_tensor(a2re[:], a2re[:], a2im[:], OP.subtract)],
        ("a2re", "a2im"), ("a2re",))
    dve(lambda v: [v.tensor_tensor(a2im[:], aare[:], cbuf["cT2im"][:, :], OP.mult)],
        ("aare", "c_cT2im"), ("a2im",))
    dve(lambda v: [v.tensor_tensor(sKm[:], aaim[:], cbuf["cT2re"][:, :], OP.mult)],
        ("aaim", "c_cT2re"), ("sKm",))
    dve(lambda v: [v.tensor_tensor(a2im[:], a2im[:], sKm[:], OP.add)],
        ("a2im", "sKm"), ("a2im",))
    A("tensor", lambda t: [
        t.matmul(psf[2][0:128, 0:32], cbuf["cBre"][:, :], a2re[:], start=True, stop=False),
        t.matmul(psf[2][0:128, 0:32], cbuf["cBimN"][:, :], a2im[:], start=False, stop=True)],
      reads=("a2re", "a2im", "c_cBre", "c_cBimN"), writes=("PS2",))
    A("vector", lambda v: [v.tensor_copy(sKm[:], psf[2][0:128, 0:32])],
      reads=("PS2",), writes=("sKm",))

    A("vector", lambda v: [v.memset(zrow[:], 0.0)], writes=("zrow",))
    A("scalar", lambda s: [s.dma_start(out=bass.AP(Kpad, 0, [[128, 1], [1, 128]]),
                                       in_=zrow[:])],
      reads=("zrow",), writes=("Kpad",), ndma=1, tag="kpad0")
    A("scalar", lambda s: [s.dma_start(out=bass.AP(Kpad, 128, [[32, 128], [1, 32]]),
                                       in_=sKm[:])],
      reads=("sKm",), writes=("Kpad",), ndma=1, tag="kpadK")

    A("scalar", lambda s: [
        s.dma_start(out=sRows32[0:31, :], in_=bass.AP(IN["Stail"], 0, [[128, 31], [1, 128]])),
        s.dma_start(out=sRows32[31:32, 0:64], in_=bass.AP(IN["Stail"], 3968, [[64, 1], [1, 64]])),
    ], writes=("sR32",), ndma=2, tag="stailld")
    A("scalar", lambda s: [s.dma_start(out=sRows32[31:32, 64:128], in_=shead[0:1, 0:64])],
      reads=("shead",), writes=("sR32",), ndma=1, tag="sheadmv")
    A("tensor", lambda t: [t.transpose(psf[6][0:128, 0:32], sRows32[:, :], cbuf["cIdf"][0:32, 0:32])],
      reads=("sR32", "c_cIdf"), writes=("PS6",))
    A("vector", lambda v: [v.tensor_copy(sScol[:], psf[6][0:128, 0:32])],
      reads=("PS6",), writes=("sScol",))
    A("scalar", lambda s: [s.dma_start(out=sRows32[:, :],
                                       in_=bass.AP(IN["bias"], 0, [[128, 32], [1, 128]]))],
      reads=("sR32",), writes=("sR32",), ndma=1, tag="biasld")
    A("tensor", lambda t: [t.transpose(psf[6][0:128, 0:32], sRows32[:, :], cbuf["cIdf"][0:32, 0:32])],
      reads=("sR32", "c_cIdf"), writes=("PS6",))
    A("vector", lambda v: [v.tensor_copy(sBcol[:], psf[6][0:128, 0:32])],
      reads=("PS6",), writes=("sBcol",))
    A("vector", lambda v: [v.tensor_relu(fd4[:], fd4[:])], reads=("fd4",), writes=("fd4",))
    A("vector", lambda v: [v.tensor_tensor(sd4[:], sd4[:], fd4[:], OP.mult)],
      reads=("sd4", "fd4"), writes=("sd4",))
    A("vector", lambda v: [v.tensor_scalar(sRsb[:], sRm[:], sd4[:, 0:1], None, OP.mult)],
      reads=("sRm", "sd4"), writes=("sRsb",))
    for ic in range(NCH):
        A("tensor", lambda t, ic=ic: [t.transpose(psb_t[0:128, 0:4],
                                                  sVr4b[0:4, ic * 128:(ic + 1) * 128],
                                                  cbuf["cIdb"][0:4, 0:4])],
          reads=("sVr4b", "c_cIdb"), writes=("PSB",))
        A("vector", lambda v, ic=ic: [v.tensor_copy(sVr[:, ic * 4:(ic + 1) * 4], psb_t[0:128, 0:4])],
          reads=("PSB",), writes=("sVr",), tag=f"vrT{ic}")
        A("tensor", lambda t, ic=ic: [t.transpose(psb_t[0:4, 0:128],
                                                  sUr32b[:, ic * 4:(ic + 1) * 4],
                                                  cbuf["cIdb"][:, :])],
          reads=("sUr32b", "c_cIdb"), writes=("PSB",))
        A("vector", lambda v: [v.tensor_copy(ur_t[:], psb_t[0:4, 0:128])],
          reads=("PSB",), writes=("ur_t",))
        A("tensor", lambda t, ic=ic: [t.matmul(psf[6][0:4, 0:128], sRsb[:], ur_t[:],
                                               start=True, stop=True)],
          reads=("sRsb", "ur_t"), writes=("PS6",))
        A("vector", lambda v, ic=ic: [v.tensor_copy(sQ[:, ic * 128:(ic + 1) * 128],
                                                    psf[6][0:4, 0:128])],
          reads=("PS6",), writes=("sQ",), tag=f"Qev{ic}")

    for h2 in range(2):
        A("scalar", lambda s, h2=h2: [
            s.dma_start(out=bass.AP(sHF, g * 1024, [[2048, 128], [128, 8], [1, 128]]),
                        in_=bass.AP(Kpad, 1 + (16 * h2 + 8 * g) * 128,
                                    [[1, 128], [128, 8], [1, 128]]))
            for g in range(2)],
          reads=("Kpad",), writes=("sHF",), ndma=2, tag=f"Hld{h2}")
        A("vector", lambda v, h2=h2: [v.tensor_copy(sHB[:], sHF[:])],
          reads=("sHF",), writes=("sHB",), tag=f"Hcast{h2}")
        for q4 in range(4):
            d0 = 16 * h2 + 4 * q4
            A("tensor", lambda t, d0=d0, h2=h2, q4=q4: [
                t.matmul(psf[q4 % 2][:, j * 128:(j + 1) * 128], cbuf["cJb"][:, :],
                         sHB[:, (4 * q4 + j) * 128:(4 * q4 + j + 1) * 128],
                         start=True, stop=True)
                for j in range(4)],
              reads=("sHB", "c_cJb"), writes=(PSK[q4 % 2],), tag=f"Gmm{d0}")
            A("vector", lambda v, d0=d0, q4=q4: [v.tensor_copy(
                sG[:, d0 * 128:(d0 + 4) * 128], psf[q4 % 2][:, 0:512])],
              reads=(PSK[q4 % 2],), writes=("G",), tag=f"Gev{d0}")





    for rb in range(n_rb):
        LAG = 16
        def emit_stats(b):
            A("tensor", lambda t, b=b: [t.matmul(psf[3][0:1, 0:RB], cbuf["cOnesColb"][:, :],
                                                 chB(b), start=(b == 0), stop=(b == NCH - 1),
                                                 skip_group_check=True)],
              reads=(f"B{b}", "c_cOnesColb"), writes=("PS3",), tag=f"summ{rb}_{b}")
            A("tensor", lambda t, b=b: [t.matmul(psf[5][0:1, 0:RB], cbuf["cOnesColb"][:, :],
                                                 sSq[b % 4][:, 0:RB], start=(b == 0),
                                                 stop=(b == NCH - 1), skip_group_check=True)],
              reads=(f"SQ{b % 4}", "c_cOnesColb"), writes=("PS5",), tag=f"sqm{rb}_{b}")
        for b in range(NCH):
            pk = b % 3
            def conv_mm(t, b=b, pk=pk):
                out = []
                for a in range(b + 1):
                    out.append(t.matmul(psf[pk][:, 0:RB], sG[:, (b - a) * 128:(b - a + 1) * 128],
                                        chA(a), start=(a == 0), stop=(a == b)))
                return out
            A("tensor", conv_mm,
              reads=("G",) + tuple(f"A{a}" for a in range(b + 1)),
              writes=(PSK[pk],), tag=f"conv{rb}_{b}")
            A("vector", lambda v, b=b, pk=pk: [v.scalar_tensor_tensor(
                chB(b), chA(b), Dp1[:, 0:1], psf[pk][:, 0:RB], OP.mult, OP.add)],
              reads=(f"A{b}", "Dp1", PSK[pk]), writes=(f"B{b}",), tag=f"cvev{rb}_{b}")
            A("scalar", lambda s, b=b: [s.activation(sSq[b % 4][:, 0:RB], chB(b), AF.Square)],
              reads=(f"B{b}",), writes=(f"SQ{b % 4}",), tag=f"sq{rb}_{b}")
            if b >= LAG:
                emit_stats(b - LAG)
        for b in range(NCH - LAG, NCH):
            emit_stats(b)
        ln_scalars_done = False
        A("vector", lambda v: [v.tensor_scalar(mrow[:], psf[3][0:1, 0:RB], 1.0 / D, None, OP.mult)],
          reads=("PS3",), writes=("mrow",), tag=f"m1{rb}")
        A("vector", lambda v: [v.tensor_scalar(srow[:], psf[5][0:1, 0:RB], 1.0 / D, None, OP.mult)],
          reads=("PS5",), writes=("srow",), tag=f"s1{rb}")
        A("vector", lambda v: [v.tensor_tensor(rstdr[:], mrow[:], mrow[:], OP.mult)],
          reads=("mrow",), writes=("rstdr",))
        A("vector", lambda v: [v.scalar_tensor_tensor(srow[:], srow[:], 1e-5, rstdr[:],
                                                      OP.add, OP.subtract)],
          reads=("srow", "rstdr"), writes=("srow",))
        A("scalar", lambda s: [s.activation(srow[:], srow[:], AF.Ln)],
          reads=("srow",), writes=("srow",))
        A("scalar", lambda s: [s.activation(rstdr[:], srow[:], AF.Exp, scale=-0.5)],
          reads=("srow",), writes=("rstdr",))
        A("vector", lambda v: [v.tensor_copy(rstd_b[:], rstdr[:])],
          reads=("rstdr",), writes=("rstd_b",))
        A("vector", lambda v: [v.tensor_tensor(srow[:], mrow[:], rstdr[:], OP.mult)],
          reads=("mrow", "rstdr"), writes=("srow",))
        A("vector", lambda v: [v.tensor_copy(amn_b[:], srow[:])],
          reads=("srow",), writes=("amn_b",))
        A("tensor", lambda t: [t.matmul(psf[0][:, 0:RB], cbuf["cOnesb"][:, :],
                                        rstd_b[0:1, 0:RB], start=True, stop=True)],
          reads=("rstd_b", "c_cOnesb"), writes=("PS0",), tag=f"bc1{rb}")
        A("tensor", lambda t: [t.matmul(psf[1][:, 0:RB], cbuf["cNOnesb"][:, :],
                                        amn_b[0:1, 0:RB], start=True, stop=True)],
          reads=("amn_b", "c_cNOnesb"), writes=("PS1",), tag=f"bc2{rb}")
        A("scalar", lambda s: [s.copy(sRstd[:, 0:RB], psf[0][:, 0:RB])],
          reads=("PS0",), writes=("sRstd",))
        A("scalar", lambda s: [s.copy(sAmn[:, 0:RB], psf[1][:, 0:RB])],
          reads=("PS1",), writes=("sAmn",))
        for ic in range(NCH):
            A("gpsimd", lambda g, ic=ic: [g.tensor_tensor(sTmp[ic % 2][:, 0:RB], chB(ic),
                                                          sRstd[:, 0:RB], OP.mult)],
              reads=(f"B{ic}", "sRstd"), writes=(f"TMP{ic % 2}",), tag=f"ap1_{rb}_{ic}")
            A("vector", lambda v, ic=ic: [v.tensor_tensor(chA(ic), sTmp[ic % 2][:, 0:RB],
                                                          sAmn[:, 0:RB], OP.add)],
              reads=(f"TMP{ic % 2}", "sAmn"), writes=(f"A{ic}",), tag=f"ap2_{rb}_{ic}")

        for rc in range(NCH):
            slot = rc % 2
            A("scalar", lambda s, rc=rc, slot=slot: [s.dma_start(
                out=sPan[slot][:], in_=bass.AP(VT, rc * D * 128, [[4096, 128], [1, 4096]]))],
              reads=tuple(f"VT{i}" for i in range(NCH)),
              writes=(f"pan{slot}",), ndma=1, tag=f"t1p{rb}_{rc}")
            pk = rc % 3
            for ic in range(NCH):
                A("tensor", lambda t, rc=rc, slot=slot, pk=pk, ic=ic: [
                    t.matmul(psf[pk][:, 0:RB],
                             sPan[slot][:, ic * 128:(ic + 1) * 128],
                             chA(ic), start=(ic == 0), stop=(ic == NCH - 1))],
                  reads=(f"pan{slot}", f"A{ic}"),
                  writes=(PSK[pk],), tag=f"t1mm{rb}_{rc}_{ic}")
            A("vector", lambda v, rc=rc, pk=pk: [v.tensor_scalar(
                chB(rc), psf[pk][:, 0:RB], sScol[:, rc:rc + 1], None, OP.mult)],
              reads=(PSK[pk], "sScol"), writes=(f"B{rc}",), tag=f"t1ev{rb}_{rc}")
        def td_mm(t):
            out = []
            for ic in range(NCH):
                out.append(t.matmul(psf[4][0:4, 0:RB], sVr[:, ic * 4:(ic + 1) * 4],
                                    chA(ic), start=(ic == 0), stop=(ic == NCH - 1)))
            return out
        A("tensor", td_mm, reads=("sVr",) + tuple(f"A{i}" for i in range(NCH)),
          writes=("PS4",), tag=f"tdmm{rb}")
        A("vector", lambda v: [v.tensor_copy(sTd[:, 0:RB], psf[4][0:4, 0:RB])],
          reads=("PS4",), writes=("sTd",), tag=f"tdev{rb}")

        def emit_stats2(oc):
            A("tensor", lambda t, oc=oc: [t.matmul(psf[3][0:1, 0:RB], cbuf["cOnesColb"][:, :],
                                                   sXq[oc % 4][:, 0:RB], start=(oc == 0),
                                                   stop=(oc == NCH - 1), skip_group_check=True)],
              reads=(f"XQ{oc % 4}", "c_cOnesColb"), writes=("PS3",), tag=f"sum2{rb}_{oc}")
            A("tensor", lambda t, oc=oc: [t.matmul(psf[5][0:1, 0:RB], cbuf["cOnesColb"][:, :],
                                                   sSq[oc % 4][:, 0:RB], start=(oc == 0),
                                                   stop=(oc == NCH - 1), skip_group_check=True)],
              reads=(f"SQ{oc % 4}", "c_cOnesColb"), writes=("PS5",), tag=f"sqm2{rb}_{oc}")
        for oc in range(NCH):
            slot = oc % 2
            A("sync", lambda s, oc=oc, slot=slot: [s.dma_start(
                out=sPan[slot][:], in_=bass.AP(UT, oc * D * 128, [[4096, 128], [1, 4096]]))],
              reads=tuple(f"UT{i}" for i in range(NCH)),
              writes=(f"pan{slot}",), ndma=1, tag=f"t2p{rb}_{oc}")
            pk = oc % 3
            def t2_mm(t, oc=oc, slot=slot, pk=pk):
                out = []
                for rc in range(NCH):
                    out.append(t.matmul(psf[pk][:, 0:RB],
                                        sPan[slot][:, rc * 128:(rc + 1) * 128],
                                        chB(rc), start=(rc == 0), stop=False))
                out.append(t.matmul(psf[pk][:, 0:RB], sQ[:, oc * 128:(oc + 1) * 128],
                                    sTd[0:4, 0:RB], start=False, stop=True))
                return out
            A("tensor", t2_mm,
              reads=(f"pan{slot}", "sQ", "sTd") + tuple(f"B{i}" for i in range(NCH)),
              writes=(PSK[pk],), tag=f"t2mm{rb}_{oc}")
            A("vector", lambda v, oc=oc, pk=pk: [v.scalar_tensor_tensor(
                sXq[oc % 4][:, 0:RB], psf[pk][:, 0:RB], sBcol[:, oc:oc + 1], chA(oc),
                OP.add, OP.add)],
              reads=(PSK[pk], "sBcol", f"A{oc}"), writes=(f"XQ{oc % 4}",), tag=f"t2ev{rb}_{oc}")
            A("scalar", lambda s, oc=oc, rb=rb: [s.dma_start(
                out=bass.AP(X2S, oc * 128 * rows + rb * RB, [[rows, 128], [1, RB]]),
                in_=sXq[oc % 4][:, 0:RB])],
              reads=(f"XQ{oc % 4}",), writes=(f"X2S{rb}_{oc}",), ndma=1, tag=f"x2st{rb}_{oc}")
            A("scalar", lambda s, oc=oc: [s.activation(sSq[oc % 4][:, 0:RB],
                                                       sXq[oc % 4][:, 0:RB], AF.Square)],
              reads=(f"XQ{oc % 4}",), writes=(f"SQ{oc % 4}",), tag=f"sq2{rb}_{oc}")
            if oc >= 3:
                emit_stats2(oc - 3)
        for oc in range(NCH - 3, NCH):
            emit_stats2(oc)
        if rb + 1 < n_rb:
            u_transposes(rb + 1)
        A("vector", lambda v: [v.tensor_scalar(mrow[:], psf[3][0:1, 0:RB], 1.0 / D, None, OP.mult)],
          reads=("PS3",), writes=("mrow",), tag=f"m2{rb}")
        A("vector", lambda v: [v.tensor_scalar(srow[:], psf[5][0:1, 0:RB], 1.0 / D, None, OP.mult)],
          reads=("PS5",), writes=("srow",), tag=f"s2{rb}")
        A("vector", lambda v: [v.tensor_tensor(rstdr[:], mrow[:], mrow[:], OP.mult)],
          reads=("mrow",), writes=("rstdr",))
        A("vector", lambda v: [v.scalar_tensor_tensor(srow[:], srow[:], 1e-5, rstdr[:],
                                                      OP.add, OP.subtract)],
          reads=("srow", "rstdr"), writes=("srow",))
        A("scalar", lambda s: [s.activation(srow[:], srow[:], AF.Ln)],
          reads=("srow",), writes=("srow",))
        A("scalar", lambda s: [s.activation(rstdr[:], srow[:], AF.Exp, scale=-0.5)],
          reads=("srow",), writes=("rstdr",))
        for g_ in range(NG):
            A("tensor", lambda t, g_=g_: [t.transpose(psf[6][0:128, 0:1],
                                                      mrow[0:1, g_ * 128:(g_ + 1) * 128],
                                                      cbuf["cIdf"][0:1, 0:1])],
              reads=("mrow", "c_cIdf"), writes=("PS6",))
            A("scalar", lambda s, g_=g_: [s.copy(m2col[:, g_:g_ + 1], psf[6][0:128, 0:1])],
              reads=("PS6",), writes=("m2col",))
            A("tensor", lambda t, g_=g_: [t.transpose(psf[6][0:128, 0:1],
                                                      rstdr[0:1, g_ * 128:(g_ + 1) * 128],
                                                      cbuf["cIdf"][0:1, 0:1])],
              reads=("rstdr", "c_cIdf"), writes=("PS6",))
            A("scalar", lambda s, g_=g_: [s.copy(r2col[:, g_:g_ + 1], psf[6][0:128, 0:1])],
              reads=("PS6",), writes=("r2col",))
        for g_ in range(NG):
            xslot = sXin[:] if g_ % 2 == 0 else sHF[:].bitcast(BF16)
            xkey = "XIN" if g_ % 2 == 0 else "sHF"
            A("sync", lambda s, g_=g_, rb=rb, xslot=xslot: [s.dma_start(
                out=xslot, in_=bass.AP(X2S, rb * RB + g_ * 128, [[rows, D], [1, 128]]),
                transpose=True)],
              reads=tuple(f"X2S{rb}_{o}" for o in range(NCH)),
              writes=(xkey,), ndma=1, tag=f"xintr{rb}_{g_}")
            for hf in range(2):
                A("vector", lambda v, g_=g_, hf=hf, xslot=xslot: [v.tensor_scalar(
                    sYout[hf][:, :], xslot[:, hf * 2048:(hf + 1) * 2048],
                    m2col[:, g_:g_ + 1], r2col[:, g_:g_ + 1], OP.subtract, OP.mult)],
                  reads=(xkey, "m2col", "r2col"), writes=(f"YO{hf}",), tag=f"ln2ap{rb}_{g_}_{hf}")
                A("sync", lambda s, g_=g_, hf=hf, rb=rb: [s.dma_start(
                    out=bass.AP(OUT, (rb * RB + g_ * 128) * D + hf * 2048, [[D, 128], [1, 2048]]),
                    in_=sYout[hf][:, :])],
                  reads=(f"YO{hf}",), writes=(f"OUT{rb}_{g_}_{hf}",), ndma=1, tag=f"outst{rb}_{g_}_{hf}")

    A("gpsimd", lambda g: [g.nop()],
      reads=tuple(f"OUT{rb}_{g_}_{hf}" for rb in range(n_rb)
                  for g_ in range(NG) for hf in range(2)), tag="final")

    P.schedule()
    sems = {}
    for s in ENG_SEM.values():
        sems[s] = es.enter_context(nc.semaphore(s))
    for q in Q_SEM.values():
        for ln in range(8):
            sems[q + str(ln)] = es.enter_context(nc.semaphore(q + str(ln)))

    by_eng = {}
    for o in P.ops:
        by_eng.setdefault(o["eng"], []).append(o)

    def run_ops(name, eng):
        for o in by_eng.get(name, ()):
            for (s, v) in o["waits"]:
                eng.wait_ge(sems[s], v)
            instrs = o["fn"](eng)
            if o["ndma"]:
                assert len(instrs) == o["ndma"], (o["tag"], len(instrs), o["ndma"])
                for ins in instrs:
                    ins.then_inc(sems[o["sem"]], 16)
            elif o["marked"]:
                instrs[-1].then_inc(sems[o["sem"]], 1)

    with nc.Block() as block:
        @block.tensor
        def _(t):
            run_ops("tensor", t)

        @block.vector
        def _(v):
            run_ops("vector", v)

        @block.scalar
        def _(s):
            run_ops("scalar", s)

        @block.gpsimd
        def _(g):
            run_ops("gpsimd", g)

        @block.sync
        def _(s):
            run_ops("sync", s)

    es.close()
    return nc


def make_inmaps(inputs, rows_per_core, ncore):
    u = np.ascontiguousarray(np.asarray(inputs["L0_attn_u"], np.float32)).reshape(-1, D)
    f32 = lambda x: np.ascontiguousarray(np.asarray(x, np.float32))
    shared = {
        "Lam": f32(inputs["L0_attn_Lambda_ri"]),
        "pq": np.concatenate([f32(inputs["L0_attn_p_ri"]).reshape(-1),
                              f32(inputs["L0_attn_q_ri"]).reshape(-1)]).reshape(1, 256),
        "Vc": f32(inputs["L0_attn_Vc_ri"]).reshape(64, 128),
        "Ct": f32(inputs["L0_attn_Ct_ri"]).reshape(1, 128),
        "Bp": f32(inputs["L0_attn_Bp"]).reshape(64, 1),
        "aD": f32(inputs["L0_attn_D"]).reshape(1, 1),
        "lstep": f32(inputs["L0_attn_log_step"]).reshape(1, 1),
        "Utop": f32(inputs["L0_ffn_U_top"]),
        "Stop": f32(inputs["L0_ffn_S_top"]).reshape(1, 64),
        "Vtop": f32(inputs["L0_ffn_Vh_top"]),
        "Utail": f32(inputs["L0_ffn_U_tail"]),
        "Stail": f32(inputs["L0_ffn_S_tail"]).reshape(-1),
        "Vtail": f32(inputs["L0_ffn_Vh_tail"]),
        "Ur": f32(inputs["L0_ffn_U_tail_r"]),
        "Str": f32(inputs["L0_ffn_S_tail_r"]).reshape(1, 4),
        "Vr4": f32(inputs["L0_ffn_Vh_tail_r"]),
        "alpha": f32(inputs["L0_ffn_alpha"]).reshape(1, 64),
        "beta": f32(inputs["L0_ffn_beta"]).reshape(1, 64),
        "fD": f32(inputs["L0_ffn_D"]).reshape(1, 4),
        "Rm": f32(inputs["L0_ffn_R"]),
        "bias": f32(inputs["L0_ffn_bias"]).reshape(-1),
    }
    for n, (k, a) in _consts().items():
        shared[n] = a
    maps = []
    for c in range(ncore):
        m = dict(shared)
        m["u"] = u[c * rows_per_core:(c + 1) * rows_per_core]
        maps.append(m)
    return maps


def kernel(**inputs):
    rows = 2048
    nc = build(rows)
    in_maps = make_inmaps(inputs, rows, NCORE)
    res = run_bass_kernel_spmd(nc, in_maps, core_ids=list(range(NCORE)))
    out = np.concatenate([np.asarray(r["OUT"], np.float32) for r in res.results], axis=0)
    return out.reshape(8, 2048, D)


# version 12
# speedup vs baseline: 1.2813x; 1.0457x over previous
import numpy as np
import ml_dtypes
from contextlib import ExitStack
import concourse.bass as bass
import concourse.mybir as mybir
from concourse.bass_utils import run_bass_kernel_spmd

F32 = mybir.dt.float32
BF16 = mybir.dt.bfloat16
AF = mybir.ActivationFunctionType
OP = mybir.AluOpType
bf16 = ml_dtypes.bfloat16

D = 4096
NCH = 32
NCORE = 8

ENG_SEM = {"tensor": "sPE", "vector": "sDV", "scalar": "sAC",
           "gpsimd": "sGP", "sync": "sSY"}
Q_SEM = {"gpsimd": "qGP", "sync": "qSY", "scalar": "qAC"}


def _consts():
    c = {}
    jj = np.arange(D, dtype=np.float64)
    Om = np.exp(-2j * np.pi * jj / D)
    T0 = (1.0 - Om) / (1.0 + Om)
    cc = 2.0 / (1.0 + Om)
    c["cT0"] = ("f", np.concatenate([T0.real, T0.imag]).astype(np.float32).reshape(1, 2 * D))
    ccm = cc.reshape(NCH, 128).T
    c["cCre"] = ("f", ccm.real.astype(np.float32).copy())
    c["cCim"] = ("f", ccm.imag.astype(np.float32).copy())
    W32 = np.exp(2j * np.pi * np.outer(np.arange(32), np.arange(32)) / 32)
    c["cAre"] = ("f", W32.real.astype(np.float32).copy())
    c["cAim"] = ("f", W32.imag.astype(np.float32).copy())
    c["cAimN"] = ("f", (-W32.imag).astype(np.float32).copy())
    T2w = np.exp(2j * np.pi * np.outer(np.arange(128), np.arange(32)) / D)
    c["cT2re"] = ("f", T2w.real.astype(np.float32).copy())
    c["cT2im"] = ("f", T2w.imag.astype(np.float32).copy())
    W128 = np.exp(2j * np.pi * np.outer(np.arange(128), np.arange(128)) / 128) / D
    c["cBre"] = ("f", W128.real.astype(np.float32).copy())
    c["cBimN"] = ("f", (-W128.imag).astype(np.float32).copy())
    c["cIdf"] = ("f", np.eye(128, dtype=np.float32))
    c["cIdb"] = ("b", np.eye(128, dtype=np.float32).astype(bf16))
    c["cJb"] = ("b", np.eye(128, dtype=np.float32)[::-1].copy().astype(bf16))
    c["cOnesf"] = ("f", np.ones((1, 128), dtype=np.float32))
    c["cNOnesf"] = ("f", np.full((1, 128), -1.0, dtype=np.float32))
    c["cOnesb"] = ("b", np.ones((1, 128), dtype=bf16))
    c["cNOnesb"] = ("b", np.full((1, 128), -1.0, np.float32).astype(bf16))
    c["cOnesColb"] = ("b", np.ones((128, 1), dtype=bf16))
    return c


class Prog:

    def __init__(self):
        self.ops = []

    def add(self, eng, fn, reads=(), writes=(), ndma=0, tag=""):
        self.ops.append(dict(eng=eng, fn=fn, reads=tuple(reads),
                             writes=tuple(writes), ndma=ndma, tag=tag))

    def schedule(self):
        writers, readers = {}, {}
        for i, o in enumerate(self.ops):
            deps = set()
            for k in o["reads"]:
                deps.update(writers.get(k, ()))
            for k in o["writes"]:
                deps.update(writers.get(k, ()))
                deps.update(readers.get(k, ()))
            deps.discard(i)
            if o["eng"] == "tensor" and not o["ndma"]:
                deps = {d for d in deps
                        if not (self.ops[d]["eng"] == "tensor" and not self.ops[d]["ndma"])}
            o["deps"] = deps
            for k in o["reads"]:
                readers.setdefault(k, []).append(i)
            for k in o["writes"]:
                writers[k] = [i]
                readers[k] = []
        NL = 8
        rrc = {}
        last_on_lane = {}
        for i, o in enumerate(self.ops):
            if o["ndma"]:
                q = Q_SEM[o["eng"]]
                lane = rrc.get(q, 0) % NL
                rrc[q] = rrc.get(q, 0) + 1
                o["sem"] = q + str(lane)
                prev = last_on_lane.get(o["sem"])
                if prev is not None:
                    o["deps"].add(prev)
                last_on_lane[o["sem"]] = i
            else:
                o["sem"] = ENG_SEM[o["eng"]]
        needed = set()
        for o in self.ops:
            needed.update(o["deps"])
        cnt = {}
        for i, o in enumerate(self.ops):
            o["marked"] = (i in needed) or bool(o["ndma"])
            if o["marked"]:
                cnt[o["sem"]] = cnt.get(o["sem"], 0) + (16 * o["ndma"] if o["ndma"] else 1)
                o["val"] = cnt[o["sem"]]
        wm = {}
        for i, o in enumerate(self.ops):
            ws = {}
            for d in o["deps"]:
                p = self.ops[d]
                assert p["marked"], (i, d, p["tag"])
                s = p["sem"]
                ws[s] = max(ws.get(s, 0), p["val"])
            final = []
            for s, v in sorted(ws.items()):
                key = (o["eng"], s)
                if wm.get(key, 0) < v:
                    wm[key] = v
                    final.append((s, v))
            o["waits"] = final


def build(rows):
    RB = min(512, rows)
    n_rb = rows // RB
    NG = RB // 128
    CONSTS = _consts()

    nc = bass.Bass("TRN2", target_bir_lowering=False)
    P = Prog()
    es = ExitStack()

    IN = {}
    def din(name, shape, dtype=F32):
        IN[name] = nc.dram_tensor(name, list(shape), dtype, kind="ExternalInput")
    din("u", (rows, D))
    din("Lam", (64, 2)); din("pq", (1, 256)); din("Vc", (64, 128))
    din("Ct", (1, 128)); din("Bp", (64, 1)); din("aD", (1, 1)); din("lstep", (1, 1))
    din("Utop", (D, 64)); din("Stop", (1, 64)); din("Vtop", (64, D))
    din("Utail", (D, 4032)); din("Stail", (4032,)); din("Vtail", (4032, D))
    din("Ur", (D, 4)); din("Str", (1, 4)); din("Vr4", (4, D))
    din("alpha", (1, 64)); din("beta", (1, 64)); din("fD", (1, 4))
    din("Rm", (4, 4)); din("bias", (4096,))
    for n, (k, a) in CONSTS.items():
        din(n, a.shape, F32 if k == "f" else BF16)

    OUT = nc.dram_tensor("OUT", [rows, D], F32, kind="ExternalOutput")
    u_tm = nc.dram_tensor("u_tm", [rows * D], BF16)
    VhPad = nc.dram_tensor("VhPad", [D * D], BF16)
    UPad = nc.dram_tensor("UPad", [D * D], BF16)
    VT = nc.dram_tensor("VT", [NCH * D * 128], BF16)
    UT = nc.dram_tensor("UT", [NCH * D * 128], BF16)
    X2S = nc.dram_tensor("X2S", [D * rows], BF16)
    Kpad = nc.dram_tensor("Kpad", [4224], F32)

    def sb(name, p, f, dtype):
        return es.enter_context(nc.sbuf_tensor(name, [p, f], dtype))
    def psum(name, p, f, dtype=F32):
        return es.enter_context(nc.psum_tensor(name, [p, f], dtype))

    sA = sb("sA", 128, NCH * RB, BF16)
    sB = sb("sB", 128, NCH * RB, BF16)
    sG = sb("sG", 128, NCH * 128, BF16)
    sPan = [sb(f"sPan{j}", 128, D, BF16) for j in range(2)]
    sXq = [sb(f"sXq{j}", 128, RB, BF16) for j in range(4)]
    sSq = [sb(f"sSq{j}", 128, RB, BF16) for j in range(4)]
    sTmp = [sb(f"sTmp{j}", 128, RB, BF16) for j in range(2)]
    sRstd = sb("sRstd", 128, RB, BF16)
    sAmn = sb("sAmn", 128, RB, BF16)
    sXin = sb("sXin", 128, D, BF16)
    sYout = [sb(f"sYout{j}", 128, 2048, F32) for j in range(2)]
    sHF = sb("sHF", 128, 2048, F32)
    sHB = sb("sHB", 128, 2048, BF16)
    sVr4b = sb("sVr4b", 4, D, BF16)
    sUr32f = sb("sUr32f", 128, 128, F32)
    sUr32b = sb("sUr32b", 128, 128, BF16)
    kv = sb("kv", 128, 512, F32)
    wood = sb("wood", 128, 32 * 12, F32)
    sQ = sb("sQ", 4, D, BF16)
    sTd = sb("sTd", 4, RB, BF16)
    sVr = sb("sVr", 128, NCH * 4, BF16)
    sScol = sb("sScol", 128, NCH, F32)
    sBcol = sb("sBcol", 128, NCH, F32)
    sRows32 = sb("sRows32", 32, 128, F32)
    m2col = sb("m2col", 128, NG, F32)
    r2col = sb("r2col", 128, NG, F32)
    mrow = sb("mrow", 1, RB, F32)
    srow = sb("srow", 1, RB, F32)
    rstdr = sb("rstdr", 1, RB, F32)
    rstd_b = sb("rstd_b", 1, RB, BF16)
    amn_b = sb("amn_b", 1, RB, BF16)
    zrow = sb("zrow", 1, 128, F32)
    Dp1 = sb("Dp1", 128, 1, F32)
    vcraw = sb("vcraw", 64, 128, F32)
    vcre = sb("vcre", 64, 64, F32); vcim = sb("vcim", 64, 64, F32)
    vctre = sb("vctre", 64, 64, F32); vctim = sb("vctim", 64, 64, F32)
    bp = sb("bp", 64, 1, F32)
    brow = sb("brow", 1, 128, F32)
    ctraw = sb("ctraw", 1, 128, F32)
    pqraw = sb("pqraw", 1, 256, F32)
    vv8 = sb("vv8", 1, 512, F32)
    vscr = sb("vscr", 1, 64, F32)
    vvt = sb("vvt", 64, 8, F32)
    lam2 = sb("lam2", 64, 2, F32)
    nlam = sb("nlam", 64, 2, F32)
    lsr = sb("lsr", 1, 1, F32)
    s2r = sb("s2r", 1, 64, F32)
    T0q = sb("T0q", 1, 2048, F32)
    denRe = sb("denRe", 64, 1024, F32)
    denIm = sb("denIm", 64, 1024, F32)
    den2 = sb("den2", 64, 1024, F32)
    dscr = sb("dscr", 64, 1024, F32)
    arre = sb("arre", 128, 32, F32); arim = sb("arim", 128, 32, F32)
    arTre = sb("arTre", 32, 128, F32); arTim = sb("arTim", 32, 128, F32)
    aare = sb("aare", 128, 32, F32); aaim = sb("aaim", 128, 32, F32)
    a2re = sb("a2re", 128, 32, F32); a2im = sb("a2im", 128, 32, F32)
    sKm = sb("sKm", 128, 32, F32)
    sal = sb("sal", 1, 64, F32); sbe = sb("sbe", 1, 64, F32)
    shead = sb("shead", 1, 64, F32)
    fd4 = sb("fd4", 4, 1, F32); sd4 = sb("sd4", 4, 1, F32)
    sRm = sb("sRm", 4, 4, F32); sRsb = sb("sRsb", 4, 4, BF16)
    ur_t = sb("ur_t", 4, 128, BF16)

    cbuf = {}
    for n, (k, a) in CONSTS.items():
        if n == "cT0":
            continue
        cbuf[n] = sb("c_" + n, a.shape[0], a.shape[1], F32 if k == "f" else BF16)

    psf = [psum(f"psf{i}", 128, 512, F32) for i in range(7)]
    psb_t = psum("psb_t", 128, 1024, BF16)
    PSK = [f"PS{i}" for i in range(7)]

    A = P.add

    def chA(ic):
        return sA[:, ic * RB:(ic + 1) * RB]
    def chB(rc):
        return sB[:, rc * RB:(rc + 1) * RB]

    def ld1(dst, src, key, tag=""):
        A("gpsimd", lambda g, d=dst, s=src: [g.dma_start(out=d, in_=s)],
          writes=(key,), ndma=1, tag=tag or ("ld-" + key))

    ld1(lam2[:], IN["Lam"][:, :], "lam2")
    ld1(pqraw[:], IN["pq"][:, :], "pqraw")
    ld1(vcraw[:], IN["Vc"][:, :], "vcraw")
    ld1(ctraw[:], IN["Ct"][:, :], "ctraw")
    ld1(bp[:], IN["Bp"][:, :], "bp")
    ld1(lsr[:], IN["lstep"][:, :], "lsr")
    ld1(sal[:], IN["alpha"][:, :], "sal")
    ld1(sbe[:], IN["beta"][:, :], "sbe")
    ld1(shead[:], IN["Stop"][:, :], "shead")
    ld1(sRm[:], IN["Rm"][:, :], "sRm")
    ld1(fd4[:], bass.AP(IN["fD"], 0, [[1, 4], [1, 1]]), "fd4")
    ld1(sd4[:], bass.AP(IN["Str"], 0, [[1, 4], [1, 1]]), "sd4")
    ld1(Dp1[0:1, 0:1], IN["aD"][:, :], "Dp1")
    for n in CONSTS:
        if n == "cT0":
            continue
        ld1(cbuf[n][:], IN[n][:, :], "c_" + n)

    nq = 4 if rows % 4 == 0 and rows >= 512 else 1
    RQ = rows // nq
    def emit_ucast(k):
        A("gpsimd", lambda g, k=k: [g.dma_start(
            out=bass.AP(u_tm, k * RQ * D, [[D, RQ], [1, D]]),
            in_=IN["u"][k * RQ:(k + 1) * RQ, :])],
          writes=(f"u_tm{k}",), ndma=1, tag=f"castu{k}")
    emit_ucast(0)

    A("gpsimd", lambda g: [g.dma_start(out=sVr4b[:], in_=IN["Vr4"][:, :])],
      writes=("sVr4b",), ndma=1, tag="vr4b")
    A("scalar", lambda s: [
        s.dma_start(out=bass.AP(sUr32f, q * 32, [[128, 128], [4, 8], [1, 4]]),
                    in_=bass.AP(IN["Ur"], q * 8 * 512, [[4, 128], [512, 8], [1, 4]]))
        for q in range(4)],
      writes=("sUr32f",), ndma=4, tag="ur32f")
    A("vector", lambda v: [v.tensor_copy(sUr32b[:], sUr32f[:])],
      reads=("sUr32f",), writes=("sUr32b",), tag="ur32cast")

    for k in range(4):
        c0, c1 = 1024 * k, 1024 * (k + 1)
        A("gpsimd", lambda g, c0=c0, c1=c1: [
            g.dma_start(out=bass.AP(VhPad, c0, [[D, 4032], [1, 1024]]),
                        in_=bass.AP(IN["Vtail"], c0, [[D, 4032], [1, 1024]])),
            g.dma_start(out=bass.AP(VhPad, 4032 * D + c0, [[D, 64], [1, 1024]]),
                        in_=bass.AP(IN["Vtop"], c0, [[D, 64], [1, 1024]])),
        ], writes=(f"VhPad{k}",), ndma=2, tag=f"castVh{k}")
    for k in range(1, nq):
        emit_ucast(k)
    for k in range(4):
        c0 = 1024 * k
        tw = 1024 if k < 3 else 4032 - 3072
        def _ucast(g, c0=c0, tw=tw, k=k):
            out = [g.dma_start(out=bass.AP(UPad, c0, [[D, D], [1, tw]]),
                               in_=bass.AP(IN["Utail"], c0, [[4032, D], [1, tw]]))]
            if k == 3:
                out.append(g.dma_start(out=bass.AP(UPad, 4032, [[D, D], [1, 64]]),
                                       in_=bass.AP(IN["Utop"], 0, [[64, D], [1, 64]])))
            return out
        A("gpsimd", _ucast, writes=(f"UPad{k}",), ndma=(2 if k == 3 else 1),
          tag=f"castU{k}")

    def u_transposes(rb):
        q0 = (rb * RB) // RQ
        q1 = ((rb + 1) * RB - 1) // RQ
        ukeys = tuple(f"u_tm{k}" for k in range(q0, q1 + 1))
        for ic in range(NCH):
            A("sync", lambda s, ic=ic, rb=rb: [s.dma_start(
                out=chA(ic),
                in_=bass.AP(u_tm, rb * RB * D + ic * 128, [[D, RB], [1, 128]]),
                transpose=True)],
              reads=ukeys, writes=(f"A{ic}",), ndma=1, tag=f"utr{rb}_{ic}")
    u_transposes(0)

    def tiled_prepass(SrcPad, Dst, dkey, tag, slot_aps, slot_keys):
        pk = "VhPad" if SrcPad is VhPad else "UPad"
        def tr(ic):
            slot = ic % 2
            A("sync", lambda s, ic=ic, slot=slot: [s.dma_start(
                out=slot_aps[slot],
                in_=bass.AP(SrcPad, ic * 128, [[D, D], [1, 128]]), transpose=True)],
              reads=(pk + str(ic // 8),),
              writes=(slot_keys[slot],), ndma=1, tag=f"{tag}tr{ic}")
        def st(ic):
            slot = ic % 2
            A("sync", lambda s, ic=ic, slot=slot: [
                s.dma_start(
                    out=bass.AP(Dst, g8 * 8 * D * 128 + ic * 128,
                                [[4096, 128], [D * 128, 8], [1, 128]]),
                    in_=bass.AP(slot_aps[slot].tensor, slot_aps[slot].offset + g8 * 1024,
                                [[D, 128], [128, 8], [1, 128]]))
                for g8 in range(4)],
              reads=(slot_keys[slot],), writes=(dkey + str(ic),), ndma=4, tag=f"{tag}st{ic}")
        for ic in range(0, NCH, 2):
            tr(ic); tr(ic + 1); st(ic); st(ic + 1)

    tiled_prepass(VhPad, VT, "VT", "vt", [sPan[0][:], sPan[1][:]], ["pan0", "pan1"])
    tiled_prepass(UPad, UT, "UT", "ut",
                  [sXin[:], sYout[0][:].bitcast(BF16)], ["XIN", "YO0"])

    A("vector", lambda v: [v.tensor_scalar(Dp1[0:1, 0:1], Dp1[0:1, 0:1], 1.0, None, OP.add)],
      reads=("Dp1",), writes=("Dp1",))
    A("tensor", lambda t: [t.matmul(psf[6][0:128, 0:1], cbuf["cOnesf"][:, :], Dp1[0:1, 0:1],
                                    start=True, stop=True)],
      reads=("Dp1", "c_cOnesf"), writes=("PS6",))
    A("scalar", lambda s: [s.copy(Dp1[:, 0:1], psf[6][0:128, 0:1])],
      reads=("PS6",), writes=("Dp1",))

    A("vector", lambda v: [v.tensor_copy(vcre[:], bass.AP(vcraw, 0, [[128, 64], [2, 64]]))],
      reads=("vcraw",), writes=("vcre",))
    A("vector", lambda v: [v.tensor_copy(vcim[:], bass.AP(vcraw, 1, [[128, 64], [2, 64]]))],
      reads=("vcraw",), writes=("vcim",))
    A("tensor", lambda t: [t.transpose(psf[4][0:64, 0:64], vcre[:], cbuf["cIdf"][0:64, 0:64])],
      reads=("vcre", "c_cIdf"), writes=("PS4",))
    A("vector", lambda v: [v.tensor_copy(vctre[:], psf[4][0:64, 0:64])],
      reads=("PS4",), writes=("vctre",))
    A("tensor", lambda t: [t.transpose(psf[4][0:64, 0:64], vcim[:], cbuf["cIdf"][0:64, 0:64])],
      reads=("vcim", "c_cIdf"), writes=("PS4",))
    A("vector", lambda v: [v.tensor_copy(vctim[:], psf[4][0:64, 0:64])],
      reads=("PS4",), writes=("vctim",))
    A("tensor", lambda t: [t.matmul(psf[5][0:1, 0:64], bp[:], vctre[:], start=True, stop=True)],
      reads=("bp", "vctre"), writes=("PS5",))
    A("vector", lambda v: [v.tensor_copy(brow[0:1, 0:64], psf[5][0:1, 0:64])],
      reads=("PS5",), writes=("brow",))
    A("tensor", lambda t: [t.matmul(psf[5][0:1, 0:64], bp[:], vctim[:], start=True, stop=True)],
      reads=("bp", "vctim"), writes=("PS5",))
    A("vector", lambda v: [v.tensor_copy(brow[0:1, 64:128], psf[5][0:1, 0:64])],
      reads=("PS5",), writes=("brow",))

    A("vector", lambda v: [v.tensor_tensor(shead[:], shead[:], sal[:], OP.mult)],
      reads=("shead", "sal"), writes=("shead",))
    A("vector", lambda v: [v.tensor_tensor(shead[:], shead[:], sbe[:], OP.add)],
      reads=("shead", "sbe"), writes=("shead",))
    A("vector", lambda v: [v.tensor_relu(shead[:], shead[:])],
      reads=("shead",), writes=("shead",))

    ctre = bass.AP(ctraw, 0, [[128, 1], [2, 64]])
    ctim = bass.AP(ctraw, 1, [[128, 1], [2, 64]])
    pre_ = bass.AP(pqraw, 0, [[256, 1], [2, 64]])
    pim_ = bass.AP(pqraw, 1, [[256, 1], [2, 64]])
    qre_ = bass.AP(pqraw, 128, [[256, 1], [2, 64]])
    qim_ = bass.AP(pqraw, 129, [[256, 1], [2, 64]])
    bre_ = brow[0:1, 0:64]
    bim_ = brow[0:1, 64:128]

    def vprod(k, are, aim, xre, xim):
        o_re = vv8[0:1, k * 64:(k + 1) * 64]
        o_im = vv8[0:1, (k + 1) * 64:(k + 2) * 64]
        src = ("ctraw", "pqraw", "brow")
        A("vector", lambda v: [v.tensor_tensor(o_re, are, xre, OP.mult)],
          reads=src, writes=(f"vv{k}",))
        A("vector", lambda v: [v.tensor_tensor(vscr[:], aim, xim, OP.mult)],
          reads=src, writes=("vscr",))
        A("vector", lambda v: [v.tensor_tensor(o_re, o_re, vscr[:], OP.add)],
          reads=(f"vv{k}", "vscr"), writes=(f"vv{k}",))
        A("vector", lambda v: [v.tensor_tensor(o_im, are, xim, OP.mult)],
          reads=src, writes=(f"vv{k + 1}",))
        A("vector", lambda v: [v.tensor_tensor(vscr[:], aim, xre, OP.mult)],
          reads=src, writes=("vscr",))
        A("vector", lambda v: [v.tensor_tensor(o_im, o_im, vscr[:], OP.subtract)],
          reads=(f"vv{k + 1}", "vscr"), writes=(f"vv{k + 1}",))

    vprod(0, ctre, ctim, bre_, bim_)
    vprod(2, ctre, ctim, pre_, pim_)
    vprod(4, qre_, qim_, bre_, bim_)
    vprod(6, qre_, qim_, pre_, pim_)

    for k in range(8):
        A("scalar", lambda s, k=k: [s.dma_start(out=sRows32[k:k + 1, 0:64],
                                                in_=vv8[0:1, k * 64:(k + 1) * 64])],
          reads=(f"vv{k}",), writes=("sR32",), ndma=1, tag=f"vvasm{k}")
    A("tensor", lambda t: [t.transpose(psf[5][0:64, 0:8], sRows32[0:8, 0:64], cbuf["cIdf"][0:8, 0:8])],
      reads=("sR32", "c_cIdf"), writes=("PS5",))
    A("vector", lambda v: [v.tensor_copy(vvt[:], psf[5][0:64, 0:8])],
      reads=("PS5",), writes=("vvt",))

    A("scalar", lambda s: [s.activation(lsr[:], lsr[:], AF.Exp)],
      reads=("lsr",), writes=("lsr",))
    A("vector", lambda v: [v.reciprocal(lsr[:], lsr[:])],
      reads=("lsr",), writes=("lsr",))
    A("vector", lambda v: [v.tensor_scalar(lsr[:], lsr[:], 2.0, None, OP.mult)],
      reads=("lsr",), writes=("lsr",))
    A("vector", lambda v: [v.tensor_scalar(s2r[:], cbuf["cOnesf"][0:1, 0:64], lsr[:], None, OP.mult)],
      reads=("c_cOnesf", "lsr"), writes=("s2r",))
    A("vector", lambda v: [v.tensor_scalar(nlam[:], lam2[:], -1.0, None, OP.mult)],
      reads=("lam2",), writes=("nlam",))

    for q in range(4):
        A("scalar", lambda s, q=q: [
            s.dma_start(out=T0q[0:1, 0:1024],
                        in_=bass.AP(IN["cT0"], q * 1024, [[2 * D, 1], [1, 1024]])),
            s.dma_start(out=T0q[0:1, 1024:2048],
                        in_=bass.AP(IN["cT0"], D + q * 1024, [[2 * D, 1], [1, 1024]])),
        ], writes=("T0q",), ndma=2, tag=f"T0q{q}")
        for h2 in range(2):
            sl = slice(h2 * 512, (h2 + 1) * 512)
            sl2 = slice(1024 + h2 * 512, 1024 + (h2 + 1) * 512)
            A("tensor", lambda t, sl=sl: [t.matmul(psf[0][0:64, 0:512], s2r[0:1, :],
                                                   T0q[0:1, sl], start=True, stop=True)],
              reads=("s2r", "T0q"), writes=("PS0",))
            A("scalar", lambda s, sl=sl: [s.activation(denRe[:, sl], psf[0][0:64, 0:512],
                                                       AF.Identity, bias=nlam[:, 0:1])],
              reads=("PS0", "nlam"), writes=("denRe",))
            A("tensor", lambda t, sl2=sl2: [t.matmul(psf[1][0:64, 0:512], s2r[0:1, :],
                                                     T0q[0:1, sl2], start=True, stop=True)],
              reads=("s2r", "T0q"), writes=("PS1",))
            A("scalar", lambda s, sl=sl: [s.activation(denIm[:, sl], psf[1][0:64, 0:512],
                                                       AF.Identity, bias=nlam[:, 1:2])],
              reads=("PS1", "nlam"), writes=("denIm",))
        A("vector", lambda v: [v.tensor_tensor(den2[:], denRe[:], denRe[:], OP.mult)],
          reads=("denRe",), writes=("den2",))
        A("vector", lambda v: [v.tensor_tensor(dscr[:], denIm[:], denIm[:], OP.mult)],
          reads=("denIm",), writes=("dscr",))
        A("vector", lambda v: [v.tensor_tensor(den2[:], den2[:], dscr[:], OP.add)],
          reads=("den2", "dscr"), writes=("den2",))
        A("vector", lambda v: [v.reciprocal(den2[:], den2[:])],
          reads=("den2",), writes=("den2",))
        A("vector", lambda v: [v.tensor_tensor(denRe[:], denRe[:], den2[:], OP.mult)],
          reads=("denRe", "den2"), writes=("denRe",))
        A("vector", lambda v: [v.scalar_tensor_tensor(denIm[:], denIm[:], -1.0, den2[:],
                                                      OP.mult, OP.mult)],
          reads=("denIm", "den2"), writes=("denIm",))
        pk = 2 + (q % 2)
        def cau_mm(t, q=q, pk=pk):
            out = []
            for tl in range(8):
                out.append(t.matmul(psf[pk][0:128, tl * 16:tl * 16 + 8],
                                    denRe[:, tl * 128:(tl + 1) * 128],
                                    vvt[:], start=True, stop=True))
                out.append(t.matmul(psf[pk][0:128, tl * 16 + 8:tl * 16 + 16],
                                    denIm[:, tl * 128:(tl + 1) * 128],
                                    vvt[:], start=True, stop=True))
            return out
        A("tensor", cau_mm, reads=("denRe", "denIm", "vvt"), writes=(PSK[pk],), tag=f"cau{q}")
        A("vector", lambda v, q=q, pk=pk: [
            v.tensor_copy(kv[:, q * 128:(q + 1) * 128], psf[pk][0:128, 0:128])],
          reads=(PSK[pk],), writes=("kv",), tag=f"cauev{q}")

    def Am(m):
        return bass.AP(kv, m, [[512, 128], [16, 32]])
    def Bm(m):
        return bass.AP(kv, 8 + m, [[512, 128], [16, 32]])
    def W(k):
        return wood[:, 32 * k:32 * (k + 1)]

    def dve(fn, reads, writes, tag=""):
        A("vector", fn, reads=reads, writes=writes, tag=tag)

    dve(lambda v: [v.tensor_tensor(W(0), Am(0), Bm(1), OP.subtract)], ("kv",), ("wood",))
    dve(lambda v: [v.tensor_tensor(W(1), Bm(0), Am(1), OP.add)], ("kv",), ("wood",))
    dve(lambda v: [v.tensor_tensor(W(2), Am(2), Bm(3), OP.subtract)], ("kv",), ("wood",))
    dve(lambda v: [v.tensor_tensor(W(3), Bm(2), Am(3), OP.add)], ("kv",), ("wood",))
    dve(lambda v: [v.tensor_tensor(W(4), Am(4), Bm(5), OP.subtract)], ("kv",), ("wood",))
    dve(lambda v: [v.tensor_tensor(W(5), Bm(4), Am(5), OP.add)], ("kv",), ("wood",))
    dve(lambda v: [v.tensor_tensor(W(6), Am(6), Bm(7), OP.subtract)], ("kv",), ("wood",))
    dve(lambda v: [v.tensor_tensor(W(7), Bm(6), Am(7), OP.add)], ("kv",), ("wood",))
    dve(lambda v: [v.tensor_tensor(W(8), W(2), W(4), OP.mult)], ("wood",), ("wood",))
    dve(lambda v: [v.tensor_tensor(W(9), W(3), W(5), OP.mult)], ("wood",), ("wood",))
    dve(lambda v: [v.tensor_tensor(W(8), W(8), W(9), OP.subtract)], ("wood",), ("wood",))
    dve(lambda v: [v.tensor_tensor(W(9), W(2), W(5), OP.mult)], ("wood",), ("wood",))
    dve(lambda v: [v.tensor_tensor(W(10), W(3), W(4), OP.mult)], ("wood",), ("wood",))
    dve(lambda v: [v.tensor_tensor(W(9), W(9), W(10), OP.add)], ("wood",), ("wood",))
    dve(lambda v: [v.tensor_scalar(W(6), W(6), 1.0, None, OP.add)], ("wood",), ("wood",))
    dve(lambda v: [v.tensor_tensor(W(10), W(6), W(6), OP.mult)], ("wood",), ("wood",))
    dve(lambda v: [v.tensor_tensor(W(11), W(7), W(7), OP.mult)], ("wood",), ("wood",))
    dve(lambda v: [v.tensor_tensor(W(10), W(10), W(11), OP.add)], ("wood",), ("wood",))
    dve(lambda v: [v.reciprocal(W(10), W(10))], ("wood",), ("wood",))
    dve(lambda v: [v.tensor_tensor(W(11), W(8), W(6), OP.mult)], ("wood",), ("wood",))
    dve(lambda v: [v.tensor_tensor(W(2), W(9), W(7), OP.mult)], ("wood",), ("wood",))
    dve(lambda v: [v.tensor_tensor(W(11), W(11), W(2), OP.add)], ("wood",), ("wood",))
    dve(lambda v: [v.tensor_tensor(W(11), W(11), W(10), OP.mult)], ("wood",), ("wood",))
    dve(lambda v: [v.tensor_tensor(W(3), W(9), W(6), OP.mult)], ("wood",), ("wood",))
    dve(lambda v: [v.tensor_tensor(W(4), W(8), W(7), OP.mult)], ("wood",), ("wood",))
    dve(lambda v: [v.tensor_tensor(W(3), W(3), W(4), OP.subtract)], ("wood",), ("wood",))
    dve(lambda v: [v.tensor_tensor(W(3), W(3), W(10), OP.mult)], ("wood",), ("wood",))
    dve(lambda v: [v.tensor_tensor(W(0), W(0), W(11), OP.subtract)], ("wood",), ("wood",))
    dve(lambda v: [v.tensor_tensor(W(1), W(1), W(3), OP.subtract)], ("wood",), ("wood",))
    dve(lambda v: [v.tensor_tensor(arre[:], cbuf["cCre"][:], W(0), OP.mult)],
        ("wood", "c_cCre"), ("arre",))
    dve(lambda v: [v.tensor_tensor(W(4), cbuf["cCim"][:], W(1), OP.mult)], ("wood", "c_cCim"), ("wood",))
    dve(lambda v: [v.tensor_tensor(arre[:], arre[:], W(4), OP.subtract)], ("arre", "wood"), ("arre",))
    dve(lambda v: [v.tensor_tensor(arim[:], cbuf["cCre"][:], W(1), OP.mult)],
        ("wood", "c_cCre"), ("arim",))
    dve(lambda v: [v.tensor_tensor(W(5), cbuf["cCim"][:], W(0), OP.mult)], ("wood", "c_cCim"), ("wood",))
    dve(lambda v: [v.tensor_tensor(arim[:], arim[:], W(5), OP.add)], ("arim", "wood"), ("arim",))

    A("tensor", lambda t: [t.transpose(psf[4][0:32, 0:128], arre[:], cbuf["cIdf"][:, :])],
      reads=("arre", "c_cIdf"), writes=("PS4",))
    A("vector", lambda v: [v.tensor_copy(arTre[:], psf[4][0:32, 0:128])],
      reads=("PS4",), writes=("arTre",))
    A("tensor", lambda t: [t.transpose(psf[4][0:32, 0:128], arim[:], cbuf["cIdf"][:, :])],
      reads=("arim", "c_cIdf"), writes=("PS4",))
    A("vector", lambda v: [v.tensor_copy(arTim[:], psf[4][0:32, 0:128])],
      reads=("PS4",), writes=("arTim",))
    A("tensor", lambda t: [
        t.matmul(psf[0][0:128, 0:32], arTre[:], cbuf["cAre"][:, :], start=True, stop=False),
        t.matmul(psf[0][0:128, 0:32], arTim[:], cbuf["cAimN"][:, :], start=False, stop=True)],
      reads=("arTre", "arTim", "c_cAre", "c_cAimN"), writes=("PS0",))
    A("vector", lambda v: [v.tensor_copy(aare[:], psf[0][0:128, 0:32])],
      reads=("PS0",), writes=("aare",))
    A("tensor", lambda t: [
        t.matmul(psf[1][0:128, 0:32], arTre[:], cbuf["cAim"][:, :], start=True, stop=False),
        t.matmul(psf[1][0:128, 0:32], arTim[:], cbuf["cAre"][:, :], start=False, stop=True)],
      reads=("arTre", "arTim", "c_cAre", "c_cAim"), writes=("PS1",))
    A("vector", lambda v: [v.tensor_copy(aaim[:], psf[1][0:128, 0:32])],
      reads=("PS1",), writes=("aaim",))
    dve(lambda v: [v.tensor_tensor(a2re[:], aare[:], cbuf["cT2re"][:, :], OP.mult)],
        ("aare", "c_cT2re"), ("a2re",))
    dve(lambda v: [v.tensor_tensor(a2im[:], aaim[:], cbuf["cT2im"][:, :], OP.mult)],
        ("aaim", "c_cT2im"), ("a2im",))
    dve(lambda v: [v.tensor_tensor(a2re[:], a2re[:], a2im[:], OP.subtract)],
        ("a2re", "a2im"), ("a2re",))
    dve(lambda v: [v.tensor_tensor(a2im[:], aare[:], cbuf["cT2im"][:, :], OP.mult)],
        ("aare", "c_cT2im"), ("a2im",))
    dve(lambda v: [v.tensor_tensor(sKm[:], aaim[:], cbuf["cT2re"][:, :], OP.mult)],
        ("aaim", "c_cT2re"), ("sKm",))
    dve(lambda v: [v.tensor_tensor(a2im[:], a2im[:], sKm[:], OP.add)],
        ("a2im", "sKm"), ("a2im",))
    A("tensor", lambda t: [
        t.matmul(psf[2][0:128, 0:32], cbuf["cBre"][:, :], a2re[:], start=True, stop=False),
        t.matmul(psf[2][0:128, 0:32], cbuf["cBimN"][:, :], a2im[:], start=False, stop=True)],
      reads=("a2re", "a2im", "c_cBre", "c_cBimN"), writes=("PS2",))
    A("vector", lambda v: [v.tensor_copy(sKm[:], psf[2][0:128, 0:32])],
      reads=("PS2",), writes=("sKm",))

    A("vector", lambda v: [v.memset(zrow[:], 0.0)], writes=("zrow",))
    A("scalar", lambda s: [s.dma_start(out=bass.AP(Kpad, 0, [[128, 1], [1, 128]]),
                                       in_=zrow[:])],
      reads=("zrow",), writes=("Kpad",), ndma=1, tag="kpad0")
    A("scalar", lambda s: [s.dma_start(out=bass.AP(Kpad, 128, [[32, 128], [1, 32]]),
                                       in_=sKm[:])],
      reads=("sKm",), writes=("Kpad",), ndma=1, tag="kpadK")

    A("scalar", lambda s: [
        s.dma_start(out=sRows32[0:31, :], in_=bass.AP(IN["Stail"], 0, [[128, 31], [1, 128]])),
        s.dma_start(out=sRows32[31:32, 0:64], in_=bass.AP(IN["Stail"], 3968, [[64, 1], [1, 64]])),
    ], writes=("sR32",), ndma=2, tag="stailld")
    A("scalar", lambda s: [s.dma_start(out=sRows32[31:32, 64:128], in_=shead[0:1, 0:64])],
      reads=("shead",), writes=("sR32",), ndma=1, tag="sheadmv")
    A("tensor", lambda t: [t.transpose(psf[6][0:128, 0:32], sRows32[:, :], cbuf["cIdf"][0:32, 0:32])],
      reads=("sR32", "c_cIdf"), writes=("PS6",))
    A("vector", lambda v: [v.tensor_copy(sScol[:], psf[6][0:128, 0:32])],
      reads=("PS6",), writes=("sScol",))
    A("scalar", lambda s: [s.dma_start(out=sRows32[:, :],
                                       in_=bass.AP(IN["bias"], 0, [[128, 32], [1, 128]]))],
      reads=("sR32",), writes=("sR32",), ndma=1, tag="biasld")
    A("tensor", lambda t: [t.transpose(psf[6][0:128, 0:32], sRows32[:, :], cbuf["cIdf"][0:32, 0:32])],
      reads=("sR32", "c_cIdf"), writes=("PS6",))
    A("vector", lambda v: [v.tensor_copy(sBcol[:], psf[6][0:128, 0:32])],
      reads=("PS6",), writes=("sBcol",))
    A("vector", lambda v: [v.tensor_relu(fd4[:], fd4[:])], reads=("fd4",), writes=("fd4",))
    A("vector", lambda v: [v.tensor_tensor(sd4[:], sd4[:], fd4[:], OP.mult)],
      reads=("sd4", "fd4"), writes=("sd4",))
    A("vector", lambda v: [v.tensor_scalar(sRsb[:], sRm[:], sd4[:, 0:1], None, OP.mult)],
      reads=("sRm", "sd4"), writes=("sRsb",))
    for ic in range(NCH):
        A("tensor", lambda t, ic=ic: [t.transpose(psb_t[0:128, 0:4],
                                                  sVr4b[0:4, ic * 128:(ic + 1) * 128],
                                                  cbuf["cIdb"][0:4, 0:4])],
          reads=("sVr4b", "c_cIdb"), writes=("PSB",))
        A("vector", lambda v, ic=ic: [v.tensor_copy(sVr[:, ic * 4:(ic + 1) * 4], psb_t[0:128, 0:4])],
          reads=("PSB",), writes=("sVr",), tag=f"vrT{ic}")
        A("tensor", lambda t, ic=ic: [t.transpose(psb_t[0:4, 0:128],
                                                  sUr32b[:, ic * 4:(ic + 1) * 4],
                                                  cbuf["cIdb"][:, :])],
          reads=("sUr32b", "c_cIdb"), writes=("PSB",))
        A("vector", lambda v: [v.tensor_copy(ur_t[:], psb_t[0:4, 0:128])],
          reads=("PSB",), writes=("ur_t",))
        A("tensor", lambda t, ic=ic: [t.matmul(psf[6][0:4, 0:128], sRsb[:], ur_t[:],
                                               start=True, stop=True)],
          reads=("sRsb", "ur_t"), writes=("PS6",))
        A("vector", lambda v, ic=ic: [v.tensor_copy(sQ[:, ic * 128:(ic + 1) * 128],
                                                    psf[6][0:4, 0:128])],
          reads=("PS6",), writes=("sQ",), tag=f"Qev{ic}")

    for h2 in range(2):
        A("scalar", lambda s, h2=h2: [
            s.dma_start(out=bass.AP(sHF, g * 1024, [[2048, 128], [128, 8], [1, 128]]),
                        in_=bass.AP(Kpad, 1 + (16 * h2 + 8 * g) * 128,
                                    [[1, 128], [128, 8], [1, 128]]))
            for g in range(2)],
          reads=("Kpad",), writes=("sHF",), ndma=2, tag=f"Hld{h2}")
        A("vector", lambda v, h2=h2: [v.tensor_copy(sHB[:], sHF[:])],
          reads=("sHF",), writes=("sHB",), tag=f"Hcast{h2}")
        for q4 in range(4):
            d0 = 16 * h2 + 4 * q4
            A("tensor", lambda t, d0=d0, h2=h2, q4=q4: [
                t.matmul(psf[q4 % 2][:, j * 128:(j + 1) * 128], cbuf["cJb"][:, :],
                         sHB[:, (4 * q4 + j) * 128:(4 * q4 + j + 1) * 128],
                         start=True, stop=True)
                for j in range(4)],
              reads=("sHB", "c_cJb"), writes=(PSK[q4 % 2],), tag=f"Gmm{d0}")
            A("vector", lambda v, d0=d0, q4=q4: [v.tensor_copy(
                sG[:, d0 * 128:(d0 + 4) * 128], psf[q4 % 2][:, 0:512])],
              reads=(PSK[q4 % 2],), writes=("G",), tag=f"Gev{d0}")





    for rb in range(n_rb):
        LAG = 16
        def emit_stats(b):
            A("tensor", lambda t, b=b: [t.matmul(psf[3][0:1, 0:RB], cbuf["cOnesColb"][:, :],
                                                 chB(b), start=(b == 0), stop=(b == NCH - 1),
                                                 skip_group_check=True)],
              reads=(f"B{b}", "c_cOnesColb"), writes=("PS3",), tag=f"summ{rb}_{b}")
            A("tensor", lambda t, b=b: [t.matmul(psf[5][0:1, 0:RB], cbuf["cOnesColb"][:, :],
                                                 sSq[b % 4][:, 0:RB], start=(b == 0),
                                                 stop=(b == NCH - 1), skip_group_check=True)],
              reads=(f"SQ{b % 4}", "c_cOnesColb"), writes=("PS5",), tag=f"sqm{rb}_{b}")
        for b in range(NCH):
            pk = b % 3
            def conv_mm(t, b=b, pk=pk):
                out = []
                for a in range(b + 1):
                    out.append(t.matmul(psf[pk][:, 0:RB], sG[:, (b - a) * 128:(b - a + 1) * 128],
                                        chA(a), start=(a == 0), stop=(a == b)))
                return out
            A("tensor", conv_mm,
              reads=("G",) + tuple(f"A{a}" for a in range(b + 1)),
              writes=(PSK[pk],), tag=f"conv{rb}_{b}")
            A("vector", lambda v, b=b, pk=pk: [v.scalar_tensor_tensor(
                chB(b), chA(b), Dp1[:, 0:1], psf[pk][:, 0:RB], OP.mult, OP.add)],
              reads=(f"A{b}", "Dp1", PSK[pk]), writes=(f"B{b}",), tag=f"cvev{rb}_{b}")
            A("scalar", lambda s, b=b: [s.activation(sSq[b % 4][:, 0:RB], chB(b), AF.Square)],
              reads=(f"B{b}",), writes=(f"SQ{b % 4}",), tag=f"sq{rb}_{b}")
            if b >= LAG:
                emit_stats(b - LAG)
        for b in range(NCH - LAG, NCH):
            emit_stats(b)
        ln_scalars_done = False
        A("vector", lambda v: [v.tensor_scalar(mrow[:], psf[3][0:1, 0:RB], 1.0 / D, None, OP.mult)],
          reads=("PS3",), writes=("mrow",), tag=f"m1{rb}")
        A("vector", lambda v: [v.tensor_scalar(srow[:], psf[5][0:1, 0:RB], 1.0 / D, None, OP.mult)],
          reads=("PS5",), writes=("srow",), tag=f"s1{rb}")
        A("vector", lambda v: [v.tensor_tensor(rstdr[:], mrow[:], mrow[:], OP.mult)],
          reads=("mrow",), writes=("rstdr",))
        A("vector", lambda v: [v.scalar_tensor_tensor(srow[:], srow[:], 1e-5, rstdr[:],
                                                      OP.add, OP.subtract)],
          reads=("srow", "rstdr"), writes=("srow",))
        A("scalar", lambda s: [s.activation(srow[:], srow[:], AF.Ln)],
          reads=("srow",), writes=("srow",))
        A("scalar", lambda s: [s.activation(rstdr[:], srow[:], AF.Exp, scale=-0.5)],
          reads=("srow",), writes=("rstdr",))
        A("vector", lambda v: [v.tensor_tensor(srow[:], mrow[:], rstdr[:], OP.mult)],
          reads=("mrow", "rstdr"), writes=("srow",))
        A("tensor", lambda t: [t.matmul(psf[3][:, 0:RB], cbuf["cOnesf"][:, :],
                                        rstdr[0:1, 0:RB], start=True, stop=True,
                                        skip_group_check=True)],
          reads=("rstdr", "c_cOnesf"), writes=("PS3",), tag=f"bc1{rb}")
        A("tensor", lambda t: [t.matmul(psf[5][:, 0:RB], cbuf["cNOnesf"][:, :],
                                        srow[0:1, 0:RB], start=True, stop=True,
                                        skip_group_check=True)],
          reads=("srow", "c_cNOnesf"), writes=("PS5",), tag=f"bc2{rb}")
        A("scalar", lambda s: [s.copy(sRstd[:, 0:RB], psf[3][:, 0:RB])],
          reads=("PS3",), writes=("sRstd",))
        for ic in range(NCH):
            A("gpsimd", lambda g, ic=ic: [g.tensor_tensor(sTmp[ic % 2][:, 0:RB], chB(ic),
                                                          sRstd[:, 0:RB], OP.mult)],
              reads=(f"B{ic}", "sRstd"), writes=(f"TMP{ic % 2}",), tag=f"ap1_{rb}_{ic}")
            A("vector", lambda v, ic=ic: [v.tensor_tensor(chA(ic), sTmp[ic % 2][:, 0:RB],
                                                          psf[5][:, 0:RB], OP.add)],
              reads=(f"TMP{ic % 2}", "PS5"), writes=(f"A{ic}",), tag=f"ap2_{rb}_{ic}")

        for rc in range(NCH):
            slot = rc % 2
            A("scalar", lambda s, rc=rc, slot=slot: [s.dma_start(
                out=sPan[slot][:], in_=bass.AP(VT, rc * D * 128, [[4096, 128], [1, 4096]]))],
              reads=tuple(f"VT{i}" for i in range(NCH)),
              writes=(f"pan{slot}",), ndma=1, tag=f"t1p{rb}_{rc}")
            pk = rc % 3
            for ic in range(NCH):
                A("tensor", lambda t, rc=rc, slot=slot, pk=pk, ic=ic: [
                    t.matmul(psf[pk][:, 0:RB],
                             sPan[slot][:, ic * 128:(ic + 1) * 128],
                             chA(ic), start=(ic == 0), stop=(ic == NCH - 1))],
                  reads=(f"pan{slot}", f"A{ic}"),
                  writes=(PSK[pk],), tag=f"t1mm{rb}_{rc}_{ic}")
            A("vector", lambda v, rc=rc, pk=pk: [v.tensor_scalar(
                chB(rc), psf[pk][:, 0:RB], sScol[:, rc:rc + 1], None, OP.mult)],
              reads=(PSK[pk], "sScol"), writes=(f"B{rc}",), tag=f"t1ev{rb}_{rc}")
        def td_mm(t):
            out = []
            for ic in range(NCH):
                out.append(t.matmul(psf[4][0:4, 0:RB], sVr[:, ic * 4:(ic + 1) * 4],
                                    chA(ic), start=(ic == 0), stop=(ic == NCH - 1)))
            return out
        A("tensor", td_mm, reads=("sVr",) + tuple(f"A{i}" for i in range(NCH)),
          writes=("PS4",), tag=f"tdmm{rb}")
        A("vector", lambda v: [v.tensor_copy(sTd[:, 0:RB], psf[4][0:4, 0:RB])],
          reads=("PS4",), writes=("sTd",), tag=f"tdev{rb}")

        def emit_stats2(oc):
            A("tensor", lambda t, oc=oc: [t.matmul(psf[3][0:1, 0:RB], cbuf["cOnesColb"][:, :],
                                                   sXq[oc % 4][:, 0:RB], start=(oc == 0),
                                                   stop=(oc == NCH - 1), skip_group_check=True)],
              reads=(f"XQ{oc % 4}", "c_cOnesColb"), writes=("PS3",), tag=f"sum2{rb}_{oc}")
            A("tensor", lambda t, oc=oc: [t.matmul(psf[5][0:1, 0:RB], cbuf["cOnesColb"][:, :],
                                                   sSq[oc % 4][:, 0:RB], start=(oc == 0),
                                                   stop=(oc == NCH - 1), skip_group_check=True)],
              reads=(f"SQ{oc % 4}", "c_cOnesColb"), writes=("PS5",), tag=f"sqm2{rb}_{oc}")
        for oc in range(NCH):
            slot = oc % 2
            A("sync", lambda s, oc=oc, slot=slot: [s.dma_start(
                out=sPan[slot][:], in_=bass.AP(UT, oc * D * 128, [[4096, 128], [1, 4096]]))],
              reads=tuple(f"UT{i}" for i in range(NCH)),
              writes=(f"pan{slot}",), ndma=1, tag=f"t2p{rb}_{oc}")
            pk = oc % 3
            def t2_mm(t, oc=oc, slot=slot, pk=pk):
                out = []
                for rc in range(NCH):
                    out.append(t.matmul(psf[pk][:, 0:RB],
                                        sPan[slot][:, rc * 128:(rc + 1) * 128],
                                        chB(rc), start=(rc == 0), stop=False))
                out.append(t.matmul(psf[pk][:, 0:RB], sQ[:, oc * 128:(oc + 1) * 128],
                                    sTd[0:4, 0:RB], start=False, stop=True))
                return out
            A("tensor", t2_mm,
              reads=(f"pan{slot}", "sQ", "sTd") + tuple(f"B{i}" for i in range(NCH)),
              writes=(PSK[pk],), tag=f"t2mm{rb}_{oc}")
            A("vector", lambda v, oc=oc, pk=pk: [v.scalar_tensor_tensor(
                sXq[oc % 4][:, 0:RB], psf[pk][:, 0:RB], sBcol[:, oc:oc + 1], chA(oc),
                OP.add, OP.add)],
              reads=(PSK[pk], "sBcol", f"A{oc}"), writes=(f"XQ{oc % 4}",), tag=f"t2ev{rb}_{oc}")
            A("scalar", lambda s, oc=oc, rb=rb: [s.dma_start(
                out=bass.AP(X2S, oc * 128 * rows + rb * RB, [[rows, 128], [1, RB]]),
                in_=sXq[oc % 4][:, 0:RB])],
              reads=(f"XQ{oc % 4}",), writes=(f"X2S{rb}_{oc}",), ndma=1, tag=f"x2st{rb}_{oc}")
            A("scalar", lambda s, oc=oc: [s.activation(sSq[oc % 4][:, 0:RB],
                                                       sXq[oc % 4][:, 0:RB], AF.Square)],
              reads=(f"XQ{oc % 4}",), writes=(f"SQ{oc % 4}",), tag=f"sq2{rb}_{oc}")
            if oc >= 3:
                emit_stats2(oc - 3)
        for oc in range(NCH - 3, NCH):
            emit_stats2(oc)
        if rb + 1 < n_rb:
            u_transposes(rb + 1)
        A("vector", lambda v: [v.tensor_scalar(mrow[:], psf[3][0:1, 0:RB], 1.0 / D, None, OP.mult)],
          reads=("PS3",), writes=("mrow",), tag=f"m2{rb}")
        A("vector", lambda v: [v.tensor_scalar(srow[:], psf[5][0:1, 0:RB], 1.0 / D, None, OP.mult)],
          reads=("PS5",), writes=("srow",), tag=f"s2{rb}")
        A("vector", lambda v: [v.tensor_tensor(rstdr[:], mrow[:], mrow[:], OP.mult)],
          reads=("mrow",), writes=("rstdr",))
        A("vector", lambda v: [v.scalar_tensor_tensor(srow[:], srow[:], 1e-5, rstdr[:],
                                                      OP.add, OP.subtract)],
          reads=("srow", "rstdr"), writes=("srow",))
        A("scalar", lambda s: [s.activation(srow[:], srow[:], AF.Ln)],
          reads=("srow",), writes=("srow",))
        A("scalar", lambda s: [s.activation(rstdr[:], srow[:], AF.Exp, scale=-0.5)],
          reads=("srow",), writes=("rstdr",))
        for g_ in range(NG):
            A("tensor", lambda t, g_=g_: [t.transpose(psf[6][0:128, 0:1],
                                                      mrow[0:1, g_ * 128:(g_ + 1) * 128],
                                                      cbuf["cIdf"][0:1, 0:1])],
              reads=("mrow", "c_cIdf"), writes=("PS6",))
            A("scalar", lambda s, g_=g_: [s.copy(m2col[:, g_:g_ + 1], psf[6][0:128, 0:1])],
              reads=("PS6",), writes=("m2col",))
            A("tensor", lambda t, g_=g_: [t.transpose(psf[6][0:128, 0:1],
                                                      rstdr[0:1, g_ * 128:(g_ + 1) * 128],
                                                      cbuf["cIdf"][0:1, 0:1])],
              reads=("rstdr", "c_cIdf"), writes=("PS6",))
            A("scalar", lambda s, g_=g_: [s.copy(r2col[:, g_:g_ + 1], psf[6][0:128, 0:1])],
              reads=("PS6",), writes=("r2col",))
        for g_ in range(NG):
            xslot = sXin[:] if g_ % 2 == 0 else sHF[:].bitcast(BF16)
            xkey = "XIN" if g_ % 2 == 0 else "sHF"
            A("sync", lambda s, g_=g_, rb=rb, xslot=xslot: [s.dma_start(
                out=xslot, in_=bass.AP(X2S, rb * RB + g_ * 128, [[rows, D], [1, 128]]),
                transpose=True)],
              reads=tuple(f"X2S{rb}_{o}" for o in range(NCH)),
              writes=(xkey,), ndma=1, tag=f"xintr{rb}_{g_}")
            for hf in range(2):
                A("vector", lambda v, g_=g_, hf=hf, xslot=xslot: [v.tensor_scalar(
                    sYout[hf][:, :], xslot[:, hf * 2048:(hf + 1) * 2048],
                    m2col[:, g_:g_ + 1], r2col[:, g_:g_ + 1], OP.subtract, OP.mult)],
                  reads=(xkey, "m2col", "r2col"), writes=(f"YO{hf}",), tag=f"ln2ap{rb}_{g_}_{hf}")
                A("sync", lambda s, g_=g_, hf=hf, rb=rb: [s.dma_start(
                    out=bass.AP(OUT, (rb * RB + g_ * 128) * D + hf * 2048, [[D, 128], [1, 2048]]),
                    in_=sYout[hf][:, :])],
                  reads=(f"YO{hf}",), writes=(f"OUT{rb}_{g_}_{hf}",), ndma=1, tag=f"outst{rb}_{g_}_{hf}")

    A("gpsimd", lambda g: [g.nop()],
      reads=tuple(f"OUT{rb}_{g_}_{hf}" for rb in range(n_rb)
                  for g_ in range(NG) for hf in range(2)), tag="final")

    P.schedule()
    sems = {}
    for s in ENG_SEM.values():
        sems[s] = es.enter_context(nc.semaphore(s))
    for q in Q_SEM.values():
        for ln in range(8):
            sems[q + str(ln)] = es.enter_context(nc.semaphore(q + str(ln)))

    by_eng = {}
    for o in P.ops:
        by_eng.setdefault(o["eng"], []).append(o)

    def run_ops(name, eng):
        for o in by_eng.get(name, ()):
            for (s, v) in o["waits"]:
                eng.wait_ge(sems[s], v)
            instrs = o["fn"](eng)
            if o["ndma"]:
                assert len(instrs) == o["ndma"], (o["tag"], len(instrs), o["ndma"])
                for ins in instrs:
                    ins.then_inc(sems[o["sem"]], 16)
            elif o["marked"]:
                instrs[-1].then_inc(sems[o["sem"]], 1)

    with nc.Block() as block:
        @block.tensor
        def _(t):
            run_ops("tensor", t)

        @block.vector
        def _(v):
            run_ops("vector", v)

        @block.scalar
        def _(s):
            run_ops("scalar", s)

        @block.gpsimd
        def _(g):
            run_ops("gpsimd", g)

        @block.sync
        def _(s):
            run_ops("sync", s)

    es.close()
    return nc


def make_inmaps(inputs, rows_per_core, ncore):
    u = np.ascontiguousarray(np.asarray(inputs["L0_attn_u"], np.float32)).reshape(-1, D)
    f32 = lambda x: np.ascontiguousarray(np.asarray(x, np.float32))
    shared = {
        "Lam": f32(inputs["L0_attn_Lambda_ri"]),
        "pq": np.concatenate([f32(inputs["L0_attn_p_ri"]).reshape(-1),
                              f32(inputs["L0_attn_q_ri"]).reshape(-1)]).reshape(1, 256),
        "Vc": f32(inputs["L0_attn_Vc_ri"]).reshape(64, 128),
        "Ct": f32(inputs["L0_attn_Ct_ri"]).reshape(1, 128),
        "Bp": f32(inputs["L0_attn_Bp"]).reshape(64, 1),
        "aD": f32(inputs["L0_attn_D"]).reshape(1, 1),
        "lstep": f32(inputs["L0_attn_log_step"]).reshape(1, 1),
        "Utop": f32(inputs["L0_ffn_U_top"]),
        "Stop": f32(inputs["L0_ffn_S_top"]).reshape(1, 64),
        "Vtop": f32(inputs["L0_ffn_Vh_top"]),
        "Utail": f32(inputs["L0_ffn_U_tail"]),
        "Stail": f32(inputs["L0_ffn_S_tail"]).reshape(-1),
        "Vtail": f32(inputs["L0_ffn_Vh_tail"]),
        "Ur": f32(inputs["L0_ffn_U_tail_r"]),
        "Str": f32(inputs["L0_ffn_S_tail_r"]).reshape(1, 4),
        "Vr4": f32(inputs["L0_ffn_Vh_tail_r"]),
        "alpha": f32(inputs["L0_ffn_alpha"]).reshape(1, 64),
        "beta": f32(inputs["L0_ffn_beta"]).reshape(1, 64),
        "fD": f32(inputs["L0_ffn_D"]).reshape(1, 4),
        "Rm": f32(inputs["L0_ffn_R"]),
        "bias": f32(inputs["L0_ffn_bias"]).reshape(-1),
    }
    for n, (k, a) in _consts().items():
        shared[n] = a
    maps = []
    for c in range(ncore):
        m = dict(shared)
        m["u"] = u[c * rows_per_core:(c + 1) * rows_per_core]
        maps.append(m)
    return maps


def kernel(**inputs):
    rows = 2048
    nc = build(rows)
    in_maps = make_inmaps(inputs, rows, NCORE)
    res = run_bass_kernel_spmd(nc, in_maps, core_ids=list(range(NCORE)))
    out = np.concatenate([np.asarray(r["OUT"], np.float32) for r in res.results], axis=0)
    return out.reshape(8, 2048, D)
